# version 09
# baseline (speedup 1.0000x reference)
import math
import numpy as np
import ml_dtypes
from contextlib import ExitStack

import concourse.bass as bass
import concourse.mybir as mybir
from concourse.bass_utils import run_bass_kernel_spmd

F32 = mybir.dt.float32
BF16 = mybir.dt.bfloat16
I32 = mybir.dt.int32
AF = mybir.ActivationFunctionType
AL = mybir.AluOpType
AX = mybir.AxisListType

B, L, D, H, HD = 4, 2048, 4096, 32, 128
TOK = 1024
NT = 8
NCH = 32
NFB = 8
EPS = 1e-5
INVSQ = 1.0 / math.sqrt(HD)

bf16 = ml_dtypes.bfloat16


class Ctr:
    def __init__(self, h):
        self.h = h
        self.n = 0

    def inc(self, instr, by=1):
        instr.then_inc(self.h, by)
        self.n += by
        return self.n


def build_nc():
    nc = bass.Bass("TRN2", target_bir_lowering=False)

    x_in = nc.dram_tensor("x_own", [NT, 128, D], F32, kind="ExternalInput").ap()
    wqT = nc.dram_tensor("wqT", [D, D], F32, kind="ExternalInput").ap()
    wkT = nc.dram_tensor("wkT", [D, D], F32, kind="ExternalInput").ap()
    wvT = nc.dram_tensor("wvT", [D, D], F32, kind="ExternalInput").ap()
    woT = nc.dram_tensor("woT", [D, D], F32, kind="ExternalInput").ap()
    wfT = nc.dram_tensor("wfT", [D, D], F32, kind="ExternalInput").ap()
    centT = nc.dram_tensor("centT", [D, 64], F32, kind="ExternalInput").ap()
    bq_pc = nc.dram_tensor("bq_pc", [128, NCH], F32, kind="ExternalInput").ap()
    bk_pc = nc.dram_tensor("bk_pc", [128, NCH], F32, kind="ExternalInput").ap()
    bv_r = nc.dram_tensor("bv_r", [1, D], F32, kind="ExternalInput").ap()
    bo_r = nc.dram_tensor("bo_r", [1, D], F32, kind="ExternalInput").ap()
    bf_r = nc.dram_tensor("bf_r", [1, D], F32, kind="ExternalInput").ap()
    assign = nc.dram_tensor("assign", [1, D], I32, kind="ExternalInput").ap()
    amask = nc.dram_tensor("amask", [128, 8, 512], BF16, kind="ExternalInput").ap()
    y_out = nc.dram_tensor("y_own", [NT, 128, D], F32, kind="ExternalOutput").ap()

    xn_d = nc.dram_tensor("xn_d", [NT, 128, D], BF16).ap()
    q_d = nc.dram_tensor("q_d", [H, 128, TOK], BF16).ap()
    k_own = nc.dram_tensor("k_own", [H, 128, TOK], BF16)
    v_own = nc.dram_tensor("v_own", [NT, 128, D], BF16)
    k_pair = nc.dram_tensor("k_pair", [2, H, 128, TOK], BF16)
    v_pair = nc.dram_tensor("v_pair", [2, NT, 128, D], BF16)
    ao_d = nc.dram_tensor("ao_d", [NT, 128, H, 128], BF16).ap()
    g_d = nc.dram_tensor("g_d", [NT, 128, D], F32).ap()
    asnrow_d = nc.dram_tensor("asnrow_d", [1, D], BF16).ap()
    r_d = nc.dram_tensor("r_d", [2, 1, 512], F32).ap()
    rsel_d = nc.dram_tensor("rsel_d", [1, D], BF16).ap()
    cnt_in = nc.dram_tensor("cnt_in", [64, 1], F32)
    cnt_out = nc.dram_tensor("cnt_out", [64, 1], F32)

    ctx = ExitStack()
    S = lambda name, shape, dt: ctx.enter_context(nc.sbuf_tensor(name, shape, dt))
    SEM = lambda name: Ctr(ctx.enter_context(nc.semaphore(name)))

    with ctx:
        ones_col = S("ones_col", [128, 1], BF16)
        ones_row = S("ones_row", [1, 128], BF16)
        ones_row_f = S("ones_row_f", [1, 128], F32)
        eps_c = S("eps_c", [128, 1], F32)
        s_bqpc = S("s_bqpc", [128, NCH], F32)
        s_bkpc = S("s_bkpc", [128, NCH], F32)
        s_brep = S("s_brep", [128, D], BF16)
        c_iota = S("c_iota", [64, 1], F32)
        s_asn32 = S("s_asn32", [128, 32], I32)
        s_asnb32 = S("s_asnb32", [128, 32], BF16)
        s_cselc = S("s_cselc", [64, 1], F32)
        s_csel = S("s_csel", [64, 1], BF16)
        s_qsel = S("s_qsel", [128, NT], F32)

        m_pe = SEM("m_pe")
        m_dve = SEM("m_dve")
        m_act = SEM("m_act")
        m_pool = SEM("m_pool")
        d_const = SEM("d_const")
        d_constg = SEM("d_constg")
        d_x = [SEM("d_x0"), SEM("d_x1")]
        d_spill = SEM("d_spill")
        d_tp = SEM("d_tp")
        d_w = [SEM("d_w0"), SEM("d_w1"), SEM("d_w2"), SEM("d_w3")]
        d_ev = [SEM(f"d_ev{i}") for i in range(4)]
        d_pf = [SEM("d_pf0"), SEM("d_pf1")]
        d_ao = [SEM("d_ao0"), SEM("d_ao1")]
        d_aot = [SEM("d_aot0"), SEM("d_aot1")]
        d_cg = [SEM("d_cg0"), SEM("d_cg1"), SEM("d_cg2")]
        d_out = [SEM("d_out0"), SEM("d_out1"), SEM("d_out2")]
        d_misc = SEM("d_misc")
        d_brow = SEM("d_brow")
        d_oh = [SEM("d_oh0"), SEM("d_oh1")]
        d_wo4 = [SEM(f"d_wo4_{i}") for i in range(4)]
        d_rr = [SEM("d_rr0"), SEM("d_rr1")]
        d_mrep = SEM("d_mrep")
        d_rsl = [SEM("d_rsl0"), SEM("d_rsl1")]
        c_k = SEM("c_k")
        c_v = SEM("c_v")
        c_ar = SEM("c_ar")

        ALL_SEMS = ([m_pe, m_dve, m_act, m_pool, d_const, d_constg] + d_x +
                    [d_spill, d_tp] + d_w + d_ev + d_pf + d_ao + d_aot + d_cg +
                    d_out + [d_misc, d_brow, c_k, c_v, c_ar] + d_oh + d_rsl + d_wo4 + d_rr + [d_mrep])
        with nc.Block() as blk0:
            @blk0.sync
            def _(sy):
                for ctr in ALL_SEMS:
                    sy.sem_clear(ctr.h)

        blk = ctx.enter_context(nc.Block())

        @blk.sync
        def _(sy):
            for dst, src in ((s_bqpc, bq_pc), (s_bkpc, bk_pc)):
                sy.dma_start(out=dst[:, :], in_=src).then_inc(d_const.h, 16)
                d_const.n += 16

        @blk.gpsimd
        def _(g):
            g.dma_start(out=s_brep[:, :], in_=bv_r.broadcast_to([128, D])).then_inc(d_brow.h, 16)
            d_brow.n += 16
            i = g.iota(c_iota[:, :], pattern=[[0, 1]], base=0, channel_multiplier=1,
                       allow_small_or_imprecise_dtypes=True)
            m_pool.inc(i)
        brep_need = d_brow.n

        @blk.sync
        def _(sy):
            sy.dma_start(out=s_asn32[:, :],
                         in_=assign.rearrange("o (p i) -> (o p) i", i=32)).then_inc(d_misc.h, 16)
            d_misc.n += 16

        @blk.vector
        def _(v):
            v.wait_ge(d_misc.h, d_misc.n)
            i = v.tensor_copy(s_asnb32[:, :], s_asn32[:, :])
            m_dve.inc(i)
        asnb_dve = m_dve.n

        @blk.sync
        def _(sy):
            sy.wait_ge(m_dve.h, asnb_dve)
            sy.dma_start(out=asnrow_d.rearrange("o (p i) -> (o p) i", i=32),
                         in_=s_asnb32[:, :]).then_inc(d_misc.h, 16)
            d_misc.n += 16
        asnrow_st = d_misc.n

        @blk.vector
        def _(v):
            v.memset(ones_col[:, :], 1.0)
            v.memset(ones_row[:, :], 1.0)
            i = v.memset(ones_row_f[:, :], 1.0)
            m_dve.inc(i)
            v.memset(eps_c[:, :], EPS)

        sw_ctx = ExitStack()
        s_w = [sw_ctx.enter_context(nc.sbuf_tensor(f"s_w{i}", [128, NCH, 512], BF16))
               for i in range(3)]

        ln_ctx = ExitStack()
        LS = lambda name, shape, dt: ln_ctx.enter_context(nc.sbuf_tensor(name, shape, dt))
        s_x = [LS("s_x0", [128, D], F32), LS("s_x1", [128, D], F32)]
        s_xc = [LS("s_xc0", [128, D], F32), LS("s_xc1", [128, D], F32)]
        s_xn = [LS("s_xn0", [128, D], BF16), LS("s_xn1", [128, D], BF16)]
        s_sum = LS("s_sum", [128, 1], F32)
        s_m = LS("s_m", [128, 1], F32)
        s_ss = LS("s_ss", [128, 1], F32)
        s_ln1 = LS("s_ln1", [128, 1], F32)
        s_rstd = LS("s_rstd", [128, 1], F32)

        ln_xc_done = [0] * NT
        ln_act_done = [0] * NT

        for j in range(NT):
            @blk.sync
            def _(sy, j=j):
                if j >= 2:
                    sy.wait_ge(m_dve.h, ln_xc_done[j - 2])
                sy.dma_start(out=s_x[j % 2][:, :], in_=x_in[j]).then_inc(d_x[j % 2].h, 16)
                d_x[j % 2].n += 16

            @blk.vector
            def _(v, j=j):
                v.wait_ge(d_x[j % 2].h, 16 * (j // 2 + 1))
                i = v.reduce_sum(s_sum[:, :], s_x[j % 2][:, :], axis=AX.X)
                m_dve.inc(i)
                v.wait_ge(m_dve.h, m_dve.n)
                i = v.tensor_scalar_mul(s_m[:, :], s_sum[:, :], 1.0 / D)
                m_dve.inc(i)
                v.wait_ge(m_dve.h, m_dve.n)
                i = v.tensor_scalar_sub(s_xc[j % 2][:, :], s_x[j % 2][:, :], s_m[:, :])
                m_dve.inc(i)

            @blk.scalar
            def _(s, j=j):
                s.wait_ge(m_dve.h, m_dve.n)
                if j >= 2:
                    s.wait_ge(d_spill.h, 16 * j)
                i = s.activation(s_xn[j % 2][:, :], s_xc[j % 2][:, :], AF.Square, accum_out=s_ss[:, :])
                m_act.inc(i)
                s.wait_ge(m_act.h, m_act.n)
                i = s.activation(s_ln1[:, :], s_ss[:, :], AF.Ln, bias=eps_c[:, :], scale=1.0 / D)
                m_act.inc(i)
                s.wait_ge(m_act.h, m_act.n)
                i = s.activation(s_rstd[:, :], s_ln1[:, :], AF.Exp, scale=-0.5)
                ln_act_done[j] = m_act.inc(i)

            @blk.vector
            def _(v, j=j):
                v.wait_ge(m_act.h, ln_act_done[j])
                if j >= 2:
                    v.wait_ge(d_spill.h, 16 * j)
                i = v.tensor_scalar_mul(s_xn[j % 2][:, :], s_xc[j % 2][:, :], s_rstd[:, :])
                ln_xc_done[j] = m_dve.inc(i)

            @blk.sync
            def _(sy, j=j):
                sy.wait_ge(m_dve.h, ln_xc_done[j])
                sy.dma_start(out=xn_d[j], in_=s_xn[j % 2][:, :]).then_inc(d_spill.h, 16)
                d_spill.n += 16

        ln_ctx.close()

        xn_ctx = ExitStack()
        xn_fm = xn_ctx.enter_context(nc.sbuf_tensor("xn_fm", [128, NCH, TOK], BF16))
        xn_flat = xn_d.rearrange("j p f -> (j p) f")

        @blk.sync
        def _(sy):
            sy.wait_ge(d_spill.h, 16 * NT)
            for c in range(NCH):
                sy.dma_start(out=xn_fm[:, c, :], in_=xn_flat[:, 128 * c:128 * (c + 1)],
                             transpose=True).then_inc(d_tp.h, 16)
                d_tp.n += 16

        pa_ctx = ExitStack()


        p_proj = [pa_ctx.enter_context(nc.psum_tensor(f"p_proj{i}", [128, 512], F32))
                  for i in range(4)]
        p_lg = pa_ctx.enter_context(nc.psum_tensor("p_lg", [128, 64], F32))
        p_cnt = pa_ctx.enter_context(nc.psum_tensor("p_cnt", [64, 1], F32))
        p_rs = pa_ctx.enter_context(nc.psum_tensor("p_rs", [1, 512], F32))
        s_oht = [pa_ctx.enter_context(nc.sbuf_tensor(f"s_oht{i}", [64, 512], BF16))
                 for i in range(2)]
        s_rsl = [pa_ctx.enter_context(nc.sbuf_tensor(f"s_rsl{i}", [1, 512], BF16))
                 for i in range(2)]
        cent_ctx = ExitStack()
        CS = lambda name, shape, dt: cent_ctx.enter_context(nc.sbuf_tensor(name, shape, dt))
        s_cent = CS("s_cent", [128, NCH, 64], BF16)
        s_cntr = CS("s_cntr", [64, 1], F32)
        s_dots = CS("s_dots", [128, 64], F32)
        s_hits = CS("s_hits", [128, 64], BF16)
        s_lm = CS("s_lm", [128, 1], F32)
        s_lb = CS("s_lb", [128, 1], F32)
        s_ld = CS("s_ld", [128, 1], F32)
        s_lt = CS("s_lt", [128, 1], F32)
        s_lmax = CS("s_lmax", [128, 1], F32)

        @blk.gpsimd
        def _(g):
            g.wait_ge(d_spill.h, 16 * NT)
            g.dma_start(out=s_cent[:, :, :],
                        in_=centT.rearrange("(c p) k -> p c k", p=128)).then_inc(d_constg.h, 16)
            d_constg.n += 16

        lg_pe_done = [0] * NT
        lg_act_done = [0] * NT
        lg_dve_done = [0] * NT
        for j in range(NT):
            @blk.tensor
            def _(t, j=j):
                if j == 0:
                    t.wait_ge(d_tp.h, 16 * NCH)
                    t.wait_ge(d_constg.h, 16)
                if j > 0:
                    t.wait_ge(m_dve.h, lg_dve_done[j - 1])
                for c in range(NCH):
                    i = t.matmul(p_lg[:, :], xn_fm[:, c, 128 * j:128 * (j + 1)],
                                 s_cent[:, c, :], start=(c == 0), stop=(c == NCH - 1))
                lg_pe_done[j] = m_pe.inc(i)

            @blk.vector
            def _(v, j=j):
                v.wait_ge(m_pe.h, lg_pe_done[j])
                i = v.reduce_max(s_lm[:, :], p_lg[:, :], axis=AX.X)
                m_dve.inc(i)
                v.wait_ge(m_dve.h, m_dve.n)
                i = v.tensor_scalar_mul(s_lb[:, :], s_lm[:, :], -10.0)
                m_dve.inc(i)

            @blk.scalar
            def _(s, j=j):
                s.wait_ge(m_dve.h, m_dve.n)
                i = s.activation(s_dots[:, :], p_lg[:, :], AF.Exp, bias=s_lb[:, :],
                                 scale=10.0, accum_out=s_ld[:, :])
                lg_act_done[j] = m_act.inc(i)

            @blk.vector
            def _(v, j=j):
                v.wait_ge(m_act.h, lg_act_done[j])
                v.reduce_max(s_lmax[:, :], s_dots[:, :], axis=AX.X)
                i = v.tensor_scalar_mul(s_lt[:, :], s_ld[:, :], 0.01)
                m_dve.inc(i)
                v.wait_ge(m_dve.h, m_dve.n)
                v.tensor_scalar(s_qsel[:, j:j + 1], s_lmax[:, :], s_lt[:, :], None, op0=AL.is_gt)
                i = v.tensor_scalar(s_hits[:, :], s_dots[:, :], s_lt[:, :], None, op0=AL.is_gt)
                lg_dve_done[j] = m_dve.inc(i)

            @blk.tensor
            def _(t, j=j):
                t.wait_ge(m_dve.h, lg_dve_done[j])
                i = t.matmul(p_cnt[:, :], s_hits[:, :], ones_col[:, :],
                             start=(j == 0), stop=(j == NT - 1))
                m_pe.inc(i)

        cnt_pe = m_pe.n

        @blk.vector
        def _(v):
            v.wait_ge(m_pe.h, cnt_pe)
            i = v.tensor_copy(s_cntr[:, :], p_cnt[:, :])
            m_dve.inc(i)

        @blk.sync
        def _(sy):
            sy.wait_ge(m_dve.h, m_dve.n)
            sy.dma_start(out=cnt_in[:, :], in_=s_cntr[:, :]).then_inc(d_misc.h, 16)
            d_misc.n += 16

        @blk.gpsimd
        def _(g):
            g.wait_ge(d_misc.h, d_misc.n)
            g.collective_compute("AllReduce", AL.add, replica_groups=[list(range(8))],
                                 ins=[cnt_in.ap().opt()], outs=[cnt_out.ap().opt()]).then_inc(c_ar.h)
            c_ar.n += 1

        cent_ctx.close()
        s_ev = [pa_ctx.enter_context(nc.sbuf_tensor(f"s_ev{i}", [128, 512], BF16)) for i in range(4)]
        s_gst = [pa_ctx.enter_context(nc.sbuf_tensor(f"s_gst{i}", [128, 512], F32)) for i in range(4)]

        pp_free = [0] * 4



        wslot_rel = [0, 0, 0]
        wseq = [0]

        def wload(src, fb):
            slot = wseq[0] % 3
            wseq[0] += 1

            @blk.gpsimd
            def _(g, src=src, fb=fb, slot=slot):
                if wslot_rel[slot] > 0:
                    g.wait_ge(m_pe.h, wslot_rel[slot])
                src_ap = src.rearrange("(c p) f -> p c f", p=128)[:, :, 512 * fb:512 * (fb + 1)]
                g.dma_start(out=s_w[slot][:, :, :], in_=src_ap).then_inc(d_w[slot].h, 16)
                d_w[slot].n += 16
            return (slot, d_w[slot].n)

        ev_gen = [0] * 4
        ev_rel = [0] * 4

        def evac_barrier():
            return [(d_ev[i], d_ev[i].n) for i in range(4)]

        def proj_fm(src_w, dst_dram, bias_pc, first=False, pre=None):
            cur_blk = pre if pre is not None else wload(src_w, 0)
            for fb in range(NFB):
                cur, need = cur_blk
                if fb + 1 < NFB:
                    nxt_blk = wload(src_w, fb + 1)
                for sub in range(4):
                    fc = fb * 4 + sub
                    pa_i, pb_i = (2 * sub) % 4, (2 * sub + 1) % 4
                    pA, pB = p_proj[pa_i], p_proj[pb_i]
                    ipair = []

                    @blk.tensor
                    def _(t, fb=fb, sub=sub, cur=cur, need=need, pA=pA, pB=pB,
                          pa_i=pa_i, pb_i=pb_i, first=first):
                        if sub == 0:
                            t.wait_ge(d_w[cur].h, need)
                            if first and fb == 0:
                                t.wait_ge(d_tp.h, 16 * NCH)
                        for ii in (pa_i, pb_i):
                            if pp_free[ii]:
                                t.wait_ge(m_dve.h, pp_free[ii])
                        for c in range(NCH):
                            lhs = s_w[cur][:, c, 128 * sub:128 * (sub + 1)]
                            t.matmul(pA[:, :], lhs, xn_fm[:, c, 0:512],
                                     start=(c == 0), stop=(c == NCH - 1))
                            i = t.matmul(pB[:, :], lhs, xn_fm[:, c, 512:1024],
                                         start=(c == 0), stop=(c == NCH - 1))
                        ipair.append(m_pe.inc(i))

                    for th, pp, ii in ((0, pA, pa_i), (1, pB, pb_i)):
                        es = (2 * fc + th) % 4

                        @blk.vector
                        def _(v, th=th, pp=pp, ii=ii, es=es, fc=fc, ipair=ipair,
                              bias_pc=bias_pc, first=first):
                            if first and fc == 0 and th == 0:
                                v.wait_ge(d_const.h, d_const.n)
                            v.wait_ge(m_pe.h, ipair[0])
                            if ev_gen[es] > 0:
                                v.wait_ge(d_ev[es].h, ev_rel[es])
                            i = v.tensor_scalar_add(s_ev[es][:, :], pp[:, :],
                                                    bias_pc[:, fc:fc + 1])
                            pp_free[ii] = m_dve.inc(i)

                        @blk.sync
                        def _(sy, th=th, es=es, fc=fc, dst_dram=dst_dram):
                            sy.wait_ge(m_dve.h, m_dve.n)
                            sy.dma_start(out=dst_dram[fc, :, 512 * th:512 * (th + 1)],
                                         in_=s_ev[es][:, :]).then_inc(d_ev[es].h, 16)
                            d_ev[es].n += 16
                            ev_rel[es] = d_ev[es].n
                            ev_gen[es] += 1
                wslot_rel[cur] = m_pe.n
                if fb + 1 < NFB:
                    cur_blk = nxt_blk

        k_ap = k_own.ap()
        proj_fm(wkT, k_ap, s_bkpc, first=True)
        kst_bar = evac_barrier()
        q_pre = wload(wqT, 0)

        @blk.gpsimd
        def _(g):
            for s_, v_ in kst_bar:
                g.wait_ge(s_.h, v_)
            g.collective_compute("AllGather", AL.bypass,
                                 replica_groups=[[0, 1], [2, 3], [4, 5], [6, 7]],
                                 ins=[k_own.ap().opt()], outs=[k_pair.ap().opt()]).then_inc(c_k.h)
            c_k.n += 1

        proj_fm(wqT, q_d, s_bqpc, pre=q_pre)
        qst_bar = evac_barrier()

        @blk.sync
        def _(sy):
            sy.wait_ge(c_ar.h, 1)
            sy.dma_start(out=s_cselc[:, :], in_=cnt_out.ap()).then_inc(d_misc.h, 16)
            d_misc.n += 16

        @blk.vector
        def _(v):
            v.wait_ge(d_misc.h, d_misc.n)
            i = v.tensor_scalar(s_csel[:, :], s_cselc[:, :], 0.5, None, op0=AL.is_gt)
            m_dve.inc(i)
        csel_dve = m_dve.n

        oh_mm = [0, 0]
        rsl_dve = [0, 0]
        for fb in range(NFB):
            sl = fb % 2

            @blk.gpsimd
            def _(g, fb=fb, sl=sl, all_misc=d_misc.n):
                if oh_mm[sl]:
                    g.wait_ge(m_pe.h, oh_mm[sl])
                else:
                    g.wait_ge(d_misc.h, all_misc)
                g.dma_start(out=s_oht[sl][:, :],
                            in_=asnrow_d[:, 512 * fb:512 * (fb + 1)].broadcast_to([64, 512])
                            ).then_inc(d_oh[sl].h, 16)
                d_oh[sl].n += 16

            @blk.vector
            def _(v, fb=fb, sl=sl):
                v.wait_ge(d_oh[sl].h, d_oh[sl].n)
                if fb == 0:
                    v.wait_ge(m_pool.h, 1)
                i = v.tensor_scalar(s_oht[sl][:, :], s_oht[sl][:, :], c_iota[:, :],
                                    None, op0=AL.is_equal)
                m_dve.inc(i)
            oh_dve = m_dve.n

            @blk.tensor
            def _(t, fb=fb, sl=sl, oh_dve=oh_dve, prev_copy=m_act.n):
                t.wait_ge(m_dve.h, oh_dve)
                if fb == 0:
                    t.wait_ge(m_dve.h, csel_dve)
                else:
                    t.wait_ge(m_act.h, prev_copy)
                i = t.matmul(p_rs[:, :], s_csel[:, :], s_oht[sl][:, :], start=True, stop=True)
                m_pe.inc(i)
            oh_mm[sl] = m_pe.n

            @blk.scalar
            def _(sc, fb=fb, sl=sl):
                sc.wait_ge(m_pe.h, oh_mm[sl])
                if rsl_dve[sl]:
                    sc.wait_ge(d_rsl[sl].h, rsl_dve[sl])
                i = sc.copy(s_rsl[sl][:, :], p_rs[:, :])
                m_act.inc(i)
            rsl_act = m_act.n

            @blk.sync
            def _(sy, fb=fb, sl=sl, rsl_act=rsl_act):
                sy.wait_ge(m_act.h, rsl_act)
                sy.dma_start(out=rsel_d[:, 512 * fb:512 * (fb + 1)],
                             in_=s_rsl[sl][:, :]).then_inc(d_rsl[sl].h, 16)
                d_rsl[sl].n += 16
                rsl_dve[sl] = d_rsl[sl].n
        rsl_bar = [(d_rsl[0], d_rsl[0].n), (d_rsl[1], d_rsl[1].n)]

        def proj_tm(src_w, dst_dram, out_f32, pre=None, brep=None, brep_need=0):
            if pre is not None:
                blocks = {0: pre[0], 1: pre[1]}
            else:
                blocks = {0: wload(src_w, 0), 1: wload(src_w, 1)}
            for wv in range(4):
                if 2 * wv + 2 < NFB:
                    blocks[2 * wv + 2] = wload(src_w, 2 * wv + 2)
                slA, nA = blocks[2 * wv]
                slB, nB = blocks[2 * wv + 1]
                for j in range(NT):
                    ui = wv * NT + j
                    idx = [(2 * ui) % 4, (2 * ui + 1) % 4]
                    ppair = [p_proj[idx[0]], p_proj[idx[1]]]
                    last = []

                    if j == 0:
                        @blk.tensor
                        def _(t, ppair=ppair, idx=idx, slA=slA, slB=slB,
                              nA=nA, nB=nB):
                            t.wait_ge(d_w[slA].h, nA)
                            for ii in idx:
                                if pp_free[ii]:
                                    t.wait_ge(m_dve.h, pp_free[ii])
                            for c in range(NCH):
                                t.matmul(ppair[0][:, :], xn_fm[:, c, 0:128], s_w[slA][:, c, :],
                                         start=(c == 0), stop=(c == NCH - 1))
                            t.wait_ge(d_w[slB].h, nB)
                            for c in range(NCH):
                                i = t.matmul(ppair[1][:, :], xn_fm[:, c, 0:128], s_w[slB][:, c, :],
                                             start=(c == 0), stop=(c == NCH - 1))
                            last.append(m_pe.inc(i))
                    else:
                        @blk.tensor
                        def _(t, j=j, ppair=ppair, idx=idx, slA=slA, slB=slB):
                            for ii in idx:
                                if pp_free[ii]:
                                    t.wait_ge(m_dve.h, pp_free[ii])
                            for c in range(NCH):
                                lhs = xn_fm[:, c, 128 * j:128 * (j + 1)]
                                t.matmul(ppair[0][:, :], lhs, s_w[slA][:, c, :],
                                         start=(c == 0), stop=(c == NCH - 1))
                                i = t.matmul(ppair[1][:, :], lhs, s_w[slB][:, c, :],
                                             start=(c == 0), stop=(c == NCH - 1))
                            last.append(m_pe.inc(i))

                    for k in range(2):
                        fb = wv * 2 + k
                        es = (2 * ui + k) % 4

                        @blk.vector
                        def _(v, k=k, fb=fb, es=es, ppair=ppair, idx=idx, last=last,
                              out_f32=out_f32, wv=wv, j=j):
                            v.wait_ge(m_pe.h, last[0])
                            if ev_gen[es] > 0:
                                v.wait_ge(d_ev[es].h, ev_rel[es])
                            if wv == 0 and j == 0 and k == 0:
                                v.wait_ge(d_brow.h, brep_need)
                            dst = s_gst[es] if out_f32 else s_ev[es]
                            i = v.tensor_add(dst[:, :], ppair[k][:, :],
                                             s_brep[:, 512 * fb:512 * (fb + 1)])
                            pp_free[idx[k]] = m_dve.inc(i)

                        if out_f32:
                            @blk.scalar
                            def _(sc, es=es):
                                sc.wait_ge(m_dve.h, m_dve.n)
                                i = sc.activation(s_gst[es][:, :], s_gst[es][:, :], AF.Silu)
                                m_act.inc(i)

                        @blk.sync
                        def _(sy, k=k, fb=fb, es=es, j=j, dst_dram=dst_dram, out_f32=out_f32):
                            if out_f32:
                                sy.wait_ge(m_act.h, m_act.n)
                            else:
                                sy.wait_ge(m_dve.h, m_dve.n)
                            src = s_gst[es] if out_f32 else s_ev[es]
                            sy.dma_start(out=dst_dram[j, :, 512 * fb:512 * (fb + 1)],
                                         in_=src[:, :]).then_inc(d_ev[es].h, 16)
                            d_ev[es].n += 16
                            ev_rel[es] = d_ev[es].n
                            ev_gen[es] += 1
                wslot_rel[slA] = m_pe.n
                wslot_rel[slB] = m_pe.n
                if 2 * wv + 3 < NFB:
                    blocks[2 * wv + 3] = wload(src_w, 2 * wv + 3)

        proj_tm(wvT, v_own.ap(), out_f32=False, brep_need=brep_need)
        vst_bar = evac_barrier()
        f_pre = (wload(wfT, 0), wload(wfT, 1))

        @blk.gpsimd
        def _(g):
            for s_, v_ in vst_bar:
                g.wait_ge(s_.h, v_)
            g.collective_compute("AllGather", AL.bypass,
                                 replica_groups=[[0, 1], [2, 3], [4, 5], [6, 7]],
                                 ins=[v_own.ap().opt()], outs=[v_pair.ap().opt()]).then_inc(c_v.h)
            c_v.n += 1

        proj_tm(wfT, g_d, out_f32=True, pre=f_pre, brep_need=brep_need)
        gst_bar = evac_barrier() + rsl_bar
        ffn_end_pe = m_pe.n
        pa_evac_dve = m_dve.n

        pa_ctx.close()
        xn_ctx.close()
        sw_ctx.close()

        @blk.gpsimd
        def _(g, pa_dve=0):
            g.wait_ge(m_dve.h, pa_evac_dve)
            g.dma_start(out=s_brep[:, :], in_=bo_r.broadcast_to([128, D])).then_inc(d_brow.h, 16)
            d_brow.n += 16
        brep_o_need = d_brow.n

        wo_early_ctx = ExitStack()
        s_wo01 = [wo_early_ctx.enter_context(nc.sbuf_tensor(f"s_wo{i}", [128, NCH, 512], BF16))
                  for i in range(2)]
        s_mrep = wo_early_ctx.enter_context(nc.sbuf_tensor("s_mrep", [128, D], BF16))

        @blk.gpsimd
        def _(g):
            for s_, v_ in rsl_bar:
                g.wait_ge(s_.h, v_)
            g.dma_start(out=s_mrep[:, :], in_=rsel_d.broadcast_to([128, D])).then_inc(d_mrep.h, 16)
            d_mrep.n += 16
        mrep_need = d_mrep.n

        @blk.gpsimd
        def _(g):
            g.wait_ge(m_pe.h, ffn_end_pe)
            for s_, v_ in gst_bar:
                g.wait_ge(s_.h, v_)
            for i_ in range(2):
                src_ap = woT.rearrange("(c p) f -> p c f", p=128)[:, :, 512 * i_:512 * (i_ + 1)]
                g.dma_start(out=s_wo01[i_][:, :, :], in_=src_ap).then_inc(d_wo4[i_].h, 16)
                d_wo4[i_].n += 16
        wo01_need = [d_wo4[0].n, d_wo4[1].n]

        at_ctx = ExitStack()
        AS = lambda name, shape, dt: at_ctx.enter_context(nc.sbuf_tensor(name, shape, dt))
        AP_ = lambda name, shape: at_ctx.enter_context(nc.psum_tensor(name, shape, F32))
        s_masks = AS("s_masks", [128, 8, 512], BF16)
        s_q = [AS("s_q0", [128, TOK], BF16), AS("s_q1", [128, TOK], BF16)]

        @blk.sync
        def _(sy):
            for s_, v_ in gst_bar:
                sy.wait_ge(s_.h, v_)
            sy.dma_start(out=s_masks[:, :, :], in_=amask).then_inc(d_misc.h, 16)
            d_misc.n += 16
        masks_need = d_misc.n
        s_k = [AS("s_k0", [128, 2048], BF16), AS("s_k1", [128, 2048], BF16)]
        s_v = [AS("s_v0", [128, 2048], BF16), AS("s_v1", [128, 2048], BF16)]
        s_exp = [AS(f"s_exp{i}", [128, 2048], BF16) for i in range(2)]
        s_ao = [AS("s_ao0", [128, TOK], BF16), AS("s_ao1", [128, TOK], BF16)]
        s_r = [AS("s_r0", [1, 512], F32), AS("s_r1", [1, 512], F32)]
        s_rr = [AS("s_rr0", [128, 512], F32), AS("s_rr1", [128, 512], F32)]
        p_s = AP_("p_s", [128, 2048])
        p_ao = [AP_("p_ao0", [128, 512]), AP_("p_ao1", [128, 512])]
        p_d = AP_("p_d", [1, 512])

        head_rel = [0, 0]
        PF_N = 5

        def prefetch_head(h):
            slot = h % 2

            @blk.sync
            def _(sy, h=h, slot=slot):
                if h <= 1:
                    sy.wait_ge(c_k.h, 1)
                    sy.wait_ge(c_v.h, 1)
                    for s_, v_ in qst_bar:
                        sy.wait_ge(s_.h, v_)
                    for s_, v_ in gst_bar:
                        sy.wait_ge(s_.h, v_)
                else:
                    sy.wait_ge(m_pe.h, head_rel[slot])
                sy.dma_start(out=s_q[slot][:, :], in_=q_d[h]).then_inc(d_pf[slot].h, 16)
                for g_ in range(2):
                    sy.dma_start(out=s_k[slot][:, 1024 * g_:1024 * (g_ + 1)],
                                 in_=k_pair.ap()[g_, h]).then_inc(d_pf[slot].h, 16)
                    vsrc = v_pair.ap()[g_, :, :, 128 * h:128 * (h + 1)].rearrange("t p f -> p t f")
                    vdst = s_v[slot][:, 1024 * g_:1024 * (g_ + 1)].rearrange("p (t f) -> p t f", f=128)
                    sy.dma_start(out=vdst, in_=vsrc).then_inc(d_pf[slot].h, 16)
                d_pf[slot].n += 16 * PF_N

        prefetch_head(0)
        prefetch_head(1)

        ps_rel_tok = [None]
        exp_slot_rel = [0, 0]
        pao_rel = [0, 0]
        ao_slot_rel = [0, 0]
        uid = 0
        LAG = 3
        pend = []
        deferred = []
        prev_r_dve = [0]
        r_slot_rel = [0, 0]
        rr_slot_rel = [0, 0]

        def finish_block(rec):
            (my, sb, qoff, exp_ready, kt0, kt1, u, aops, slot, npair, ao_i, bq, b_id, hh) = rec
            rb = b_id % 2
            pe_block_done = m_pe.n

            @blk.vector
            def _(v, rb=rb, pe_block_done=pe_block_done):
                v.wait_ge(m_pe.h, pe_block_done)
                if r_slot_rel[rb]:
                    v.wait_ge(d_rr[rb].h, r_slot_rel[rb])
                i = v.reciprocal(s_r[rb][:, :], p_d[:, :])
                m_dve.inc(i)
            r_dve = m_dve.n
            prev_r_dve[0] = r_dve

            @blk.sync
            def _(sy, rb=rb, r_dve=r_dve, rr_rel=rr_slot_rel[rb]):
                sy.wait_ge(m_dve.h, r_dve)
                sy.dma_start(out=r_d[rb], in_=s_r[rb][:, :]).then_inc(d_rr[rb].h, 16)
                d_rr[rb].n += 16
                sy.wait_ge(d_rr[rb].h, d_rr[rb].n)
                if rr_rel:
                    sy.wait_ge(m_dve.h, rr_rel)
                sy.dma_start(out=s_rr[rb][:, :],
                             in_=r_d[rb].broadcast_to([128, 512])).then_inc(d_rr[rb].h, 16)
                d_rr[rb].n += 16
            rr_need = d_rr[rb].n
            r_slot_rel[rb] = rr_need

            def emit_evac(rb=rb, rr_need=rr_need, aops=aops, slot=slot, bq=bq,
                          ao_i=ao_i, hh=hh):
                @blk.vector
                def _(v):
                    v.wait_ge(d_rr[rb].h, rr_need)
                    if ao_slot_rel[slot]:
                        v.wait_ge(d_ao[slot].h, ao_slot_rel[slot])
                    i = v.tensor_mul(s_ao[slot][:, 512 * bq:512 * (bq + 1)],
                                     aops[:, :], s_rr[rb][:, :])
                    m_dve.inc(i)
                pao_rel[ao_i] = m_dve.n
                rr_slot_rel[rb] = m_dve.n

                if bq == 1:
                    ao_dve = m_dve.n

                    @blk.sync
                    def _(sy):
                        sy.wait_ge(m_dve.h, ao_dve)
                        sy.dma_start(out=ao_d[:, :, hh, :].rearrange("j p f -> p j f"),
                                     in_=s_ao[slot][:, :].rearrange(
                                         "p (j f) -> p j f", f=128)).then_inc(d_ao[slot].h, 16)
                        d_ao[slot].n += 16
                        ao_slot_rel[slot] = d_ao[slot].n
            deferred.append(emit_evac)

        def emit_dav(rec):
            (my, sb, qoff, exp_ready, kt0, kt1, u, aops, slot, npair, ao_i, bq, b_id, hh) = rec

            @blk.tensor
            def _(t):
                t.wait_ge(exp_ready[0].h, exp_ready[1])
                if u == 0:
                    if pao_rel[ao_i]:
                        t.wait_ge(m_dve.h, pao_rel[ao_i])
                    if prev_r_dve[0]:
                        t.wait_ge(m_dve.h, prev_r_dve[0])
                for half in (0, 1):
                    kk = 2 * u + half
                    t.matmul(p_d[:, :], ones_col[:, :],
                             s_exp[sb][:, qoff + 512 * half:qoff + 512 * (half + 1)],
                             start=(kk == 0), stop=(kk == 2 * npair - 1))
                for half, (g_, tt_) in ((0, kt0), (1, kt1)):
                    kk = 2 * u + half
                    col = 1024 * g_ + 128 * tt_
                    i = t.matmul(aops[:, :], s_v[slot][:, col:col + 128],
                                 s_exp[sb][:, qoff + 512 * half:qoff + 512 * (half + 1)],
                                 start=(kk == 0), stop=(kk == 2 * npair - 1))
                m_pe.inc(i)
            exp_slot_rel[sb] = m_pe.n

            if u == npair - 1:
                finish_block(rec)
                if bq == 1:
                    head_rel[hh % 2] = m_pe.n
                    if hh + 2 < H:
                        prefetch_head(hh + 2)

        b_id = 0
        batch = 0
        for h in range(H):
            slot = h % 2
            for Bq in range(2):
                kts = [(g_, t) for t in range(4 * (Bq + 1)) for g_ in (0, 1)]
                npair = len(kts) // 2
                ao_i = (2 * h + Bq) % 2
                aops = p_ao[ao_i]

                for u0 in range(0, npair, 2):
                    recs = []
                    sb = batch % 2
                    for u in (u0, u0 + 1):
                        pos = u - u0
                        qoff = 1024 * pos
                        kt0, kt1 = kts[2 * u], kts[2 * u + 1]
                        my = uid
                        uid += 1

                        @blk.tensor
                        def _(t, h=h, Bq=Bq, u=u, pos=pos, slot=slot, kt0=kt0, kt1=kt1,
                              qoff=qoff):
                            if u == 0 and Bq == 0:
                                t.wait_ge(d_pf[slot].h, 16 * PF_N * (h // 2 + 1))
                            if pos == 0:
                                if ps_rel_tok[0] is not None:
                                    ctr, val = ps_rel_tok[0]
                                    t.wait_ge(ctr.h, val)
                                else:
                                    t.wait_ge(m_dve.h, pa_evac_dve)
                            for half, (g_, tt_) in ((0, kt0), (1, kt1)):
                                col = 1024 * g_ + 128 * tt_
                                i = t.matmul(p_s[:, qoff + 512 * half:qoff + 512 * (half + 1)],
                                             s_k[slot][:, col:col + 128],
                                             s_q[slot][:, 512 * Bq:512 * (Bq + 1)],
                                             start=True, stop=True)
                            m_pe.inc(i)

                        recs.append([my, sb, qoff, None, kt0, kt1, u, aops, slot,
                                     npair, ao_i, Bq, b_id, h])

                        while deferred:
                            deferred.pop(0)()
                        if len(pend) >= LAG:
                            emit_dav(pend.pop(0))

                    pe_scores = m_pe.n

                    @blk.scalar
                    def _(s, sb=sb, batch=batch, pe_scores=pe_scores):
                        s.wait_ge(m_pe.h, pe_scores)
                        if batch >= 2:
                            s.wait_ge(m_pe.h, exp_slot_rel[sb])
                        i = s.activation(s_exp[sb][:, :], p_s[:, :], AF.Exp, scale=INVSQ)
                        m_act.inc(i)
                    ps_rel_tok[0] = (m_act, m_act.n)
                    exp_tok = (m_act, m_act.n)

                    for rec in recs:
                        my, sb_, qoff, _, kt0, kt1, u, *_rest = rec
                        exp_ready = exp_tok
                        masked = []
                        for half, (g_, tt_) in ((0, kt0), (1, kt1)):
                            if tt_ >= 4 * Bq:
                                masked.append((half, g_ * 4 + (tt_ - 4 * Bq)))
                        if masked:
                            @blk.vector
                            def _(v, sb_=sb_, qoff=qoff, masked=masked, exp_tok=exp_tok,
                                  first_mask=(batch == 0 and u == 0)):
                                if first_mask:
                                    v.wait_ge(d_misc.h, masks_need)
                                v.wait_ge(exp_tok[0].h, exp_tok[1])
                                for half, mi in masked:
                                    i = v.tensor_mul(
                                        s_exp[sb_][:, qoff + 512 * half:qoff + 512 * (half + 1)],
                                        s_exp[sb_][:, qoff + 512 * half:qoff + 512 * (half + 1)],
                                        s_masks[:, mi, :])
                                m_dve.inc(i)
                            exp_ready = (m_dve, m_dve.n)
                        rec[3] = exp_ready
                        pend.append(tuple(rec))
                    batch += 1
                b_id += 1

        while pend:
            emit_dav(pend.pop(0))
            while deferred:
                deferred.pop(0)()
        while deferred:
            deferred.pop(0)()

        aot_all_bar = [(d_ao[i], d_ao[i].n) for i in range(2)]
        at_ctx.close()

        attn_end_dve = m_dve.n

        wo_ctx = ExitStack()
        WS = lambda name, shape, dt: wo_ctx.enter_context(nc.sbuf_tensor(name, shape, dt))
        s_wo = s_wo01 + [WS(f"s_wo{i}", [128, NCH, 512], BF16) for i in range(2, 4)]
        s_aot = [WS("s_aot0", [128, H, 128], BF16), WS("s_aot1", [128, H, 128], BF16)]
        NCR = 3
        s_g2 = [WS(f"s_g2{i}", [128, 512], F32) for i in range(NCR)]
        s_x2 = [WS(f"s_x2{i}", [128, 512], F32) for i in range(NCR)]
        s_sil = [WS(f"s_sil{i}", [128, 512], F32) for i in range(NCR)]
        s_sv = [WS(f"s_sv{i}", [128, 512], F32) for i in range(NCR)]
        p_wo = [wo_ctx.enter_context(nc.psum_tensor(f"p_wo{i}", [128, 512], F32))
                for i in range(4)]

        wo_rel = [0, 0, 0, 0]

        def load_woblock(fb, slot):
            @blk.gpsimd
            def _(g, fb=fb, slot=slot):
                if wo_rel[slot] == 0:
                    g.wait_ge(m_dve.h, attn_end_dve)
                    for s_, v_ in aot_all_bar:
                        g.wait_ge(s_.h, v_)
                else:
                    g.wait_ge(m_pe.h, wo_rel[slot])
                src_ap = woT.rearrange("(c p) f -> p c f", p=128)[:, :, 512 * fb:512 * (fb + 1)]
                g.dma_start(out=s_wo[slot][:, :, :], in_=src_ap).then_inc(d_wo4[slot].h, 16)
                d_wo4[slot].n += 16
            return d_wo4[slot].n

        aot_gen = [0, 0]
        aot_rel = [0, 0]
        cg_gen = [0] * NCR
        cg_rel = [0] * NCR
        out_slot_rel = [0, 0, 0]
        sil_rel = [0] * NCR
        sv_rel = [0] * NCR
        pwo_free = [0] * 4

        wo_need = {0: wo01_need[0], 1: wo01_need[1]}
        cri = 0

        for wv in range(4):
            sl0, sl1 = (2 * wv) % 4, (2 * wv + 1) % 4
            if 2 * wv + 2 < NFB:
                wo_need[(2 * wv + 2) % 4] = load_woblock(2 * wv + 2, (2 * wv + 2) % 4)
            if 2 * wv + 3 < NFB:
                wo_need[(2 * wv + 3) % 4] = load_woblock(2 * wv + 3, (2 * wv + 3) % 4)
            for j in range(NT):
                aslot = j % 2

                @blk.scalar
                def _(sy, wv=wv, j=j, aslot=aslot):
                    if aot_gen[aslot] > 0:
                        sy.wait_ge(m_pe.h, aot_rel[aslot])
                    else:
                        sy.wait_ge(m_dve.h, attn_end_dve)
                        for s_, v_ in aot_all_bar:
                            sy.wait_ge(s_.h, v_)
                    sy.dma_start(out=s_aot[aslot][:, :, :],
                                 in_=ao_d[j]).then_inc(d_aot[aslot].h, 16)
                    d_aot[aslot].n += 16

                aot_gen[aslot] += 1
                aot_need = d_aot[aslot].n

                crs = [cri % NCR, (cri + 1) % NCR]
                cri += 2
                cg_need = {}
                for k in range(2):
                    fb = 2 * wv + k
                    cs = crs[k]

                    @blk.scalar
                    def _(sy, j=j, fb=fb, cs=cs, wv=wv):
                        if cg_gen[cs] > 0:
                            sy.wait_ge(m_pool.h, cg_rel[cs])
                        else:
                            sy.wait_ge(m_dve.h, attn_end_dve)
                            for s_, v_ in aot_all_bar:
                                sy.wait_ge(s_.h, v_)
                            for s_, v_ in gst_bar:
                                sy.wait_ge(s_.h, v_)
                        sy.dma_start(out=s_g2[cs][:, :],
                                     in_=g_d[j, :, 512 * fb:512 * (fb + 1)]).then_inc(d_cg[cs].h, 16)
                        sy.dma_start(out=s_x2[cs][:, :],
                                     in_=x_in[j, :, 512 * fb:512 * (fb + 1)]).then_inc(d_cg[cs].h, 16)
                        d_cg[cs].n += 32
                    cg_gen[cs] += 1
                    cg_need[cs] = d_cg[cs].n

                pA, pB = p_wo[(2 * j) % 4], p_wo[(2 * j + 1) % 4]
                iA, iB = (2 * j) % 4, (2 * j + 1) % 4

                @blk.tensor
                def _(t, wv=wv, j=j, aslot=aslot, sl0=sl0, sl1=sl1, pA=pA, pB=pB,
                      iA=iA, iB=iB, aot_need=aot_need,
                      n0=wo_need[sl0], n1=wo_need[sl1]):
                    if j == 0:
                        t.wait_ge(d_wo4[sl0].h, n0)
                        t.wait_ge(d_wo4[sl1].h, n1)
                        if wv == 0:
                            t.wait_ge(m_dve.h, attn_end_dve)
                    t.wait_ge(d_aot[aslot].h, aot_need)
                    for ii in (iA, iB):
                        if pwo_free[ii]:
                            t.wait_ge(m_dve.h, pwo_free[ii])
                    for c in range(NCH):
                        lhs = s_aot[aslot][:, c, :]
                        t.matmul(pA[:, :], lhs, s_wo[sl0][:, c, :],
                                 start=(c == 0), stop=(c == NCH - 1))
                        i = t.matmul(pB[:, :], lhs, s_wo[sl1][:, c, :],
                                     start=(c == 0), stop=(c == NCH - 1))
                    m_pe.inc(i)
                aot_rel[aslot] = m_pe.n
                wo_pe = m_pe.n

                sv_ready = {}
                for k, (pp, ip) in enumerate(((pA, iA), (pB, iB))):
                    fb = 2 * wv + k
                    os_ = crs[k]

                    @blk.vector
                    def _(v, pp=pp, fb=fb, os_=os_, wo_pe=wo_pe):
                        v.wait_ge(m_pe.h, wo_pe)
                        if sv_rel[os_]:
                            v.wait_ge(m_pool.h, sv_rel[os_])
                        if out_slot_rel[os_]:
                            v.wait_ge(d_out[os_].h, out_slot_rel[os_])
                        if wv == 0 and j == 0:
                            v.wait_ge(d_brow.h, brep_o_need)
                        i = v.tensor_add(s_sv[os_][:, :], pp[:, :],
                                         s_brep[:, 512 * fb:512 * (fb + 1)])
                        m_dve.inc(i)
                    pwo_free[ip] = m_dve.n
                    sv_ready[k] = m_dve.n

                for k in range(2):
                    fb = 2 * wv + k
                    cs = crs[k]
                    os_ = cs

                    @blk.vector
                    def _(v, j=j, fb=fb, os_=os_, cs=cs, cg_need=cg_need[cs],
                          first=(wv == 0 and j == 0 and k == 0)):
                        v.wait_ge(d_cg[cs].h, cg_need)
                        if sil_rel[os_]:
                            v.wait_ge(m_pool.h, sil_rel[os_])
                        if first:
                            v.wait_ge(d_mrep.h, mrep_need)
                        i = v.scalar_tensor_tensor(s_sil[os_][:, :], s_g2[cs][:, :],
                                                   s_qsel[:, j:j + 1],
                                                   s_mrep[:, 512 * fb:512 * (fb + 1)],
                                                   op0=AL.mult, op1=AL.mult)
                        m_dve.inc(i)
                    dve_cmb = m_dve.n

                    @blk.gpsimd
                    def _(g, os_=os_, cs=cs, dve_cmb=dve_cmb, sv_ready=sv_ready[k]):
                        g.wait_ge(m_dve.h, dve_cmb)
                        g.wait_ge(m_dve.h, sv_ready)
                        i = g.tensor_mul(s_sv[os_][:, :], s_sv[os_][:, :], s_sil[os_][:, :])
                        m_pool.inc(i)
                        sil_rel[os_] = m_pool.n
                        g.wait_ge(m_pool.h, m_pool.n)
                        i = g.tensor_add(s_sv[os_][:, :], s_sv[os_][:, :], s_x2[cs][:, :])
                        m_pool.inc(i)
                    sv_rel[os_] = m_pool.n
                    cg_rel[cs] = m_pool.n
                    pool_cmb = m_pool.n

                    @blk.sync
                    def _(sy, j=j, fb=fb, os_=os_, pool_cmb=pool_cmb):
                        sy.wait_ge(m_pool.h, pool_cmb)
                        sy.dma_start(out=y_out[j, :, 512 * fb:512 * (fb + 1)],
                                     in_=s_sv[os_][:, :]).then_inc(d_out[os_].h, 16)
                        d_out[os_].n += 16
                        out_slot_rel[os_] = d_out[os_].n

            wo_rel[sl0] = m_pe.n
            wo_rel[sl1] = m_pe.n

        @blk.sync
        def _(sy):
            for i_ in range(3):
                sy.wait_ge(d_out[i_].h, d_out[i_].n)

        wo_ctx.close()
        wo_early_ctx.close()

    return nc


_NC_CACHE = None


def _get_nc():
    global _NC_CACHE
    if _NC_CACHE is None:
        _NC_CACHE = build_nc()
    return _NC_CACHE


def _build_masks(r):
    kk = np.arange(128)[:, None]
    qq = np.arange(512)[None, :]
    out = np.zeros((8, 128, 512), np.float32)
    for g in range(2):
        for m in range(4):
            delta = 1 if (g == 1 and r == 0) else 0
            out[g * 4 + m] = (kk <= qq - 128 * m - delta).astype(np.float32)
    return np.ascontiguousarray(out.transpose(1, 0, 2)).astype(bf16)


def kernel(L0_attn_x, L0_attn_Wq, L0_attn_bq, L0_attn_Wk, L0_attn_bk,
           L0_attn_Wv, L0_attn_bv, L0_attn_Wo, L0_attn_bo,
           L0_ffn_weight, L0_ffn_bias, L0_ffn_centroids, L0_ffn_assignments):
    x = np.ascontiguousarray(np.asarray(L0_attn_x, dtype=np.float32))
    wqT = np.ascontiguousarray(np.asarray(L0_attn_Wq, np.float32).T)
    wkT = np.ascontiguousarray(np.asarray(L0_attn_Wk, np.float32).T)
    wvT = np.ascontiguousarray(np.asarray(L0_attn_Wv, np.float32).T)
    woT = np.ascontiguousarray(np.asarray(L0_attn_Wo, np.float32).T)
    wfT = np.ascontiguousarray(np.asarray(L0_ffn_weight, np.float32).T)
    centT = np.ascontiguousarray(np.asarray(L0_ffn_centroids, np.float32).T)
    bq = np.asarray(L0_attn_bq, np.float32)
    bk = np.asarray(L0_attn_bk, np.float32)
    bv = np.ascontiguousarray(np.asarray(L0_attn_bv, np.float32).reshape(1, D))
    bo = np.ascontiguousarray(np.asarray(L0_attn_bo, np.float32).reshape(1, D))
    bf_ = np.ascontiguousarray(np.asarray(L0_ffn_bias, np.float32).reshape(1, D))
    assign = np.ascontiguousarray(np.asarray(L0_ffn_assignments, np.int32).reshape(1, D))
    bq_pc = np.ascontiguousarray(bq.reshape(NCH, 128).T)
    bk_pc = np.ascontiguousarray(bk.reshape(NCH, 128).T)

    nc = _get_nc()
    in_maps = []
    for c in range(8):
        b, r = c // 2, c % 2
        x_own = np.ascontiguousarray(x[b, r::2, :]).reshape(NT, 128, D)
        in_maps.append({
            "x_own": x_own, "wqT": wqT, "wkT": wkT, "wvT": wvT, "woT": woT,
            "wfT": wfT, "centT": centT, "bq_pc": bq_pc, "bk_pc": bk_pc,
            "bv_r": bv, "bo_r": bo, "bf_r": bf_, "assign": assign,
            "amask": _build_masks(r),
        })
    res = run_bass_kernel_spmd(nc, in_maps, core_ids=list(range(8)))
    out = np.empty((B, L, D), np.float32)
    for c in range(8):
        b, r = c // 2, c % 2
        out[b, r::2, :] = res.results[c]["y_own"].reshape(TOK, D)
    return out


# version 10
# speedup vs baseline: 1.1077x; 1.1077x over previous
import math
import numpy as np
import ml_dtypes
from contextlib import ExitStack

import concourse.bass as bass
import concourse.mybir as mybir
from concourse.bass_utils import run_bass_kernel_spmd

F32 = mybir.dt.float32
BF16 = mybir.dt.bfloat16
I32 = mybir.dt.int32
AF = mybir.ActivationFunctionType
AL = mybir.AluOpType
AX = mybir.AxisListType

B, L, D, H, HD = 4, 2048, 4096, 32, 128
TOK = 1024
NT = 8
NCH = 32
NFB = 8
EPS = 1e-5
INVSQ = 1.0 / math.sqrt(HD)

bf16 = ml_dtypes.bfloat16


class Ctr:
    def __init__(self, h):
        self.h = h
        self.n = 0

    def inc(self, instr, by=1):
        instr.then_inc(self.h, by)
        self.n += by
        return self.n


def build_nc():
    nc = bass.Bass("TRN2", target_bir_lowering=False)

    x_in = nc.dram_tensor("x_own", [NT, 128, D], F32, kind="ExternalInput").ap()
    wqT = nc.dram_tensor("wqT", [D, D], F32, kind="ExternalInput").ap()
    wkT = nc.dram_tensor("wkT", [D, D], F32, kind="ExternalInput").ap()
    wvT = nc.dram_tensor("wvT", [D, D], F32, kind="ExternalInput").ap()
    woT = nc.dram_tensor("woT", [D, D], F32, kind="ExternalInput").ap()
    wfT = nc.dram_tensor("wfT", [D, D], F32, kind="ExternalInput").ap()
    centT = nc.dram_tensor("centT", [D, 64], F32, kind="ExternalInput").ap()
    bq_pc = nc.dram_tensor("bq_pc", [128, NCH], F32, kind="ExternalInput").ap()
    bk_pc = nc.dram_tensor("bk_pc", [128, NCH], F32, kind="ExternalInput").ap()
    bv_r = nc.dram_tensor("bv_r", [1, D], F32, kind="ExternalInput").ap()
    bo_r = nc.dram_tensor("bo_r", [1, D], F32, kind="ExternalInput").ap()
    bf_r = nc.dram_tensor("bf_r", [1, D], F32, kind="ExternalInput").ap()
    assign = nc.dram_tensor("assign", [1, D], I32, kind="ExternalInput").ap()
    amask = nc.dram_tensor("amask", [128, 8, 512], BF16, kind="ExternalInput").ap()
    y_out = nc.dram_tensor("y_own", [NT, 128, D], F32, kind="ExternalOutput").ap()

    xn_d = nc.dram_tensor("xn_d", [NT, 128, D], BF16).ap()
    q_d = nc.dram_tensor("q_d", [H, 128, TOK], BF16).ap()
    k_own = nc.dram_tensor("k_own", [H, 128, TOK], BF16)
    v_own = nc.dram_tensor("v_own", [NT, 128, D], BF16)
    k_pair = nc.dram_tensor("k_pair", [2, H, 128, TOK], BF16)
    v_pair = nc.dram_tensor("v_pair", [2, NT, 128, D], BF16)
    ao_d = nc.dram_tensor("ao_d", [NT, 128, H, 128], BF16).ap()
    g_d = nc.dram_tensor("g_d", [NT, 128, D], F32).ap()
    asnrow_d = nc.dram_tensor("asnrow_d", [1, D], BF16).ap()
    r_d = nc.dram_tensor("r_d", [2, 1, 512], F32).ap()
    rsel_d = nc.dram_tensor("rsel_d", [1, D], BF16).ap()
    cnt_in = nc.dram_tensor("cnt_in", [64, 1], F32)
    cnt_out = nc.dram_tensor("cnt_out", [64, 1], F32)

    ctx = ExitStack()
    S = lambda name, shape, dt: ctx.enter_context(nc.sbuf_tensor(name, shape, dt))
    SEM = lambda name: Ctr(ctx.enter_context(nc.semaphore(name)))

    with ctx:
        ones_col = S("ones_col", [128, 1], BF16)
        ones_row = S("ones_row", [1, 128], BF16)
        ones_row_f = S("ones_row_f", [1, 128], F32)
        eps_c = S("eps_c", [128, 1], F32)
        s_bqpc = S("s_bqpc", [128, NCH], F32)
        s_bkpc = S("s_bkpc", [128, NCH], F32)
        s_brep = S("s_brep", [128, D], BF16)
        c_iota = S("c_iota", [64, 1], F32)
        s_asn32 = S("s_asn32", [128, 32], I32)
        s_asnb32 = S("s_asnb32", [128, 32], BF16)
        s_cselc = S("s_cselc", [64, 1], F32)
        s_csel = S("s_csel", [64, 1], BF16)
        s_qsel = S("s_qsel", [128, NT], F32)

        m_pe = SEM("m_pe")
        m_dve = SEM("m_dve")
        m_act = SEM("m_act")
        m_pool = SEM("m_pool")
        d_const = SEM("d_const")
        d_constg = SEM("d_constg")
        d_x = [SEM("d_x0"), SEM("d_x1")]
        d_spill = SEM("d_spill")
        d_tp = SEM("d_tp")
        d_w = [SEM("d_w0"), SEM("d_w1"), SEM("d_w2"), SEM("d_w3")]
        d_ev = [SEM(f"d_ev{i}") for i in range(4)]
        d_pf = [SEM("d_pf0"), SEM("d_pf1")]
        d_ao = [SEM("d_ao0"), SEM("d_ao1")]
        d_aot = [SEM("d_aot0"), SEM("d_aot1")]
        d_cg = [SEM("d_cg0"), SEM("d_cg1"), SEM("d_cg2")]
        d_out = [SEM("d_out0"), SEM("d_out1"), SEM("d_out2")]
        d_misc = SEM("d_misc")
        d_brow = SEM("d_brow")
        d_oh = [SEM("d_oh0"), SEM("d_oh1")]
        d_wo4 = [SEM(f"d_wo4_{i}") for i in range(4)]
        d_rr = [SEM("d_rr0"), SEM("d_rr1")]
        d_mrep = SEM("d_mrep")
        d_rsl = [SEM("d_rsl0"), SEM("d_rsl1")]
        c_k = SEM("c_k")
        c_v = SEM("c_v")
        c_ar = SEM("c_ar")

        ALL_SEMS = ([m_pe, m_dve, m_act, m_pool, d_const, d_constg] + d_x +
                    [d_spill, d_tp] + d_w + d_ev + d_pf + d_ao + d_aot + d_cg +
                    d_out + [d_misc, d_brow, c_k, c_v, c_ar] + d_oh + d_rsl + d_wo4 + d_rr + [d_mrep])
        with nc.Block() as blk0:
            @blk0.sync
            def _(sy):
                for ctr in ALL_SEMS:
                    sy.sem_clear(ctr.h)

        blk = ctx.enter_context(nc.Block())

        @blk.sync
        def _(sy):
            for dst, src in ((s_bqpc, bq_pc), (s_bkpc, bk_pc)):
                sy.dma_start(out=dst[:, :], in_=src).then_inc(d_const.h, 16)
                d_const.n += 16

        @blk.gpsimd
        def _(g):
            g.dma_start(out=s_brep[:, :], in_=bv_r.broadcast_to([128, D])).then_inc(d_brow.h, 16)
            d_brow.n += 16
            i = g.iota(c_iota[:, :], pattern=[[0, 1]], base=0, channel_multiplier=1,
                       allow_small_or_imprecise_dtypes=True)
            m_pool.inc(i)
        brep_need = d_brow.n

        @blk.sync
        def _(sy):
            sy.dma_start(out=s_asn32[:, :],
                         in_=assign.rearrange("o (p i) -> (o p) i", i=32)).then_inc(d_misc.h, 16)
            d_misc.n += 16

        @blk.vector
        def _(v):
            v.wait_ge(d_misc.h, d_misc.n)
            i = v.tensor_copy(s_asnb32[:, :], s_asn32[:, :])
            m_dve.inc(i)
        asnb_dve = m_dve.n

        @blk.sync
        def _(sy):
            sy.wait_ge(m_dve.h, asnb_dve)
            sy.dma_start(out=asnrow_d.rearrange("o (p i) -> (o p) i", i=32),
                         in_=s_asnb32[:, :]).then_inc(d_misc.h, 16)
            d_misc.n += 16
        asnrow_st = d_misc.n

        @blk.vector
        def _(v):
            v.memset(ones_col[:, :], 1.0)
            v.memset(ones_row[:, :], 1.0)
            i = v.memset(ones_row_f[:, :], 1.0)
            m_dve.inc(i)
            v.memset(eps_c[:, :], EPS)

        sw_ctx = ExitStack()
        s_w = [sw_ctx.enter_context(nc.sbuf_tensor(f"s_w{i}", [128, NCH, 512], BF16))
               for i in range(3)]

        ln_ctx = ExitStack()
        LS = lambda name, shape, dt: ln_ctx.enter_context(nc.sbuf_tensor(name, shape, dt))
        s_x = [LS("s_x0", [128, D], F32), LS("s_x1", [128, D], F32)]
        s_xc = [LS("s_xc0", [128, D], F32), LS("s_xc1", [128, D], F32)]
        s_xn = [LS("s_xn0", [128, D], BF16), LS("s_xn1", [128, D], BF16)]
        s_sum = LS("s_sum", [128, 1], F32)
        s_m = LS("s_m", [128, 1], F32)
        s_ss = LS("s_ss", [128, 1], F32)
        s_ln1 = LS("s_ln1", [128, 1], F32)
        s_rstd = LS("s_rstd", [128, 1], F32)

        ln_xc_done = [0] * NT
        ln_act_done = [0] * NT

        for j in range(NT):
            @blk.sync
            def _(sy, j=j):
                if j >= 2:
                    sy.wait_ge(m_dve.h, ln_xc_done[j - 2])
                sy.dma_start(out=s_x[j % 2][:, :], in_=x_in[j]).then_inc(d_x[j % 2].h, 16)
                d_x[j % 2].n += 16

            @blk.vector
            def _(v, j=j):
                v.wait_ge(d_x[j % 2].h, 16 * (j // 2 + 1))
                i = v.reduce_sum(s_sum[:, :], s_x[j % 2][:, :], axis=AX.X)
                m_dve.inc(i)
                v.wait_ge(m_dve.h, m_dve.n)
                i = v.tensor_scalar_mul(s_m[:, :], s_sum[:, :], 1.0 / D)
                m_dve.inc(i)
                v.wait_ge(m_dve.h, m_dve.n)
                i = v.tensor_scalar_sub(s_xc[j % 2][:, :], s_x[j % 2][:, :], s_m[:, :])
                m_dve.inc(i)

            @blk.scalar
            def _(s, j=j):
                s.wait_ge(m_dve.h, m_dve.n)
                if j >= 2:
                    s.wait_ge(d_spill.h, 16 * j)
                i = s.activation(s_xn[j % 2][:, :], s_xc[j % 2][:, :], AF.Square, accum_out=s_ss[:, :])
                m_act.inc(i)
                s.wait_ge(m_act.h, m_act.n)
                i = s.activation(s_ln1[:, :], s_ss[:, :], AF.Ln, bias=eps_c[:, :], scale=1.0 / D)
                m_act.inc(i)
                s.wait_ge(m_act.h, m_act.n)
                i = s.activation(s_rstd[:, :], s_ln1[:, :], AF.Exp, scale=-0.5)
                ln_act_done[j] = m_act.inc(i)

            @blk.vector
            def _(v, j=j):
                v.wait_ge(m_act.h, ln_act_done[j])
                if j >= 2:
                    v.wait_ge(d_spill.h, 16 * j)
                i = v.tensor_scalar_mul(s_xn[j % 2][:, :], s_xc[j % 2][:, :], s_rstd[:, :])
                ln_xc_done[j] = m_dve.inc(i)

            @blk.sync
            def _(sy, j=j):
                sy.wait_ge(m_dve.h, ln_xc_done[j])
                sy.dma_start(out=xn_d[j], in_=s_xn[j % 2][:, :]).then_inc(d_spill.h, 16)
                d_spill.n += 16

        ln_ctx.close()

        xn_ctx = ExitStack()
        xn_fm = xn_ctx.enter_context(nc.sbuf_tensor("xn_fm", [128, NCH, TOK], BF16))
        xn_flat = xn_d.rearrange("j p f -> (j p) f")

        @blk.sync
        def _(sy):
            sy.wait_ge(d_spill.h, 16 * NT)
            for c in range(NCH):
                sy.dma_start(out=xn_fm[:, c, :], in_=xn_flat[:, 128 * c:128 * (c + 1)],
                             transpose=True).then_inc(d_tp.h, 16)
                d_tp.n += 16

        pa_ctx = ExitStack()


        p_proj = [pa_ctx.enter_context(nc.psum_tensor(f"p_proj{i}", [128, 512], F32))
                  for i in range(4)]
        p_lg = pa_ctx.enter_context(nc.psum_tensor("p_lg", [128, 64], F32))
        p_cnt = pa_ctx.enter_context(nc.psum_tensor("p_cnt", [64, 1], F32))
        p_rs = pa_ctx.enter_context(nc.psum_tensor("p_rs", [1, 512], F32))
        s_oht = [pa_ctx.enter_context(nc.sbuf_tensor(f"s_oht{i}", [64, 512], BF16))
                 for i in range(2)]
        s_rsl = [pa_ctx.enter_context(nc.sbuf_tensor(f"s_rsl{i}", [1, 512], BF16))
                 for i in range(2)]
        cent_ctx = ExitStack()
        CS = lambda name, shape, dt: cent_ctx.enter_context(nc.sbuf_tensor(name, shape, dt))
        s_cent = CS("s_cent", [128, NCH, 64], BF16)
        s_cntr = CS("s_cntr", [64, 1], F32)
        s_dots = CS("s_dots", [128, 64], F32)
        s_hits = CS("s_hits", [128, 64], BF16)
        s_lm = CS("s_lm", [128, 1], F32)
        s_lb = CS("s_lb", [128, 1], F32)
        s_ld = CS("s_ld", [128, 1], F32)
        s_lt = CS("s_lt", [128, 1], F32)
        s_lmax = CS("s_lmax", [128, 1], F32)

        @blk.gpsimd
        def _(g):
            g.wait_ge(d_spill.h, 16 * NT)
            g.dma_start(out=s_cent[:, :, :],
                        in_=centT.rearrange("(c p) k -> p c k", p=128)).then_inc(d_constg.h, 16)
            d_constg.n += 16

        lg_pe_done = [0] * NT
        lg_act_done = [0] * NT
        lg_dve_done = [0] * NT
        for j in range(NT):
            @blk.tensor
            def _(t, j=j):
                if j == 0:
                    t.wait_ge(d_tp.h, 16 * NCH)
                    t.wait_ge(d_constg.h, 16)
                if j > 0:
                    t.wait_ge(m_dve.h, lg_dve_done[j - 1])
                for c in range(NCH):
                    i = t.matmul(p_lg[:, :], xn_fm[:, c, 128 * j:128 * (j + 1)],
                                 s_cent[:, c, :], start=(c == 0), stop=(c == NCH - 1))
                lg_pe_done[j] = m_pe.inc(i)

            @blk.vector
            def _(v, j=j):
                v.wait_ge(m_pe.h, lg_pe_done[j])
                i = v.reduce_max(s_lm[:, :], p_lg[:, :], axis=AX.X)
                m_dve.inc(i)
                v.wait_ge(m_dve.h, m_dve.n)
                i = v.tensor_scalar_mul(s_lb[:, :], s_lm[:, :], -10.0)
                m_dve.inc(i)

            @blk.scalar
            def _(s, j=j):
                s.wait_ge(m_dve.h, m_dve.n)
                i = s.activation(s_dots[:, :], p_lg[:, :], AF.Exp, bias=s_lb[:, :],
                                 scale=10.0, accum_out=s_ld[:, :])
                lg_act_done[j] = m_act.inc(i)

            @blk.vector
            def _(v, j=j):
                v.wait_ge(m_act.h, lg_act_done[j])
                v.reduce_max(s_lmax[:, :], s_dots[:, :], axis=AX.X)
                i = v.tensor_scalar_mul(s_lt[:, :], s_ld[:, :], 0.01)
                m_dve.inc(i)
                v.wait_ge(m_dve.h, m_dve.n)
                v.tensor_scalar(s_qsel[:, j:j + 1], s_lmax[:, :], s_lt[:, :], None, op0=AL.is_gt)
                i = v.tensor_scalar(s_hits[:, :], s_dots[:, :], s_lt[:, :], None, op0=AL.is_gt)
                lg_dve_done[j] = m_dve.inc(i)

            @blk.tensor
            def _(t, j=j):
                t.wait_ge(m_dve.h, lg_dve_done[j])
                i = t.matmul(p_cnt[:, :], s_hits[:, :], ones_col[:, :],
                             start=(j == 0), stop=(j == NT - 1))
                m_pe.inc(i)

        cnt_pe = m_pe.n

        @blk.vector
        def _(v):
            v.wait_ge(m_pe.h, cnt_pe)
            i = v.tensor_copy(s_cntr[:, :], p_cnt[:, :])
            m_dve.inc(i)

        @blk.sync
        def _(sy):
            sy.wait_ge(m_dve.h, m_dve.n)
            sy.dma_start(out=cnt_in[:, :], in_=s_cntr[:, :]).then_inc(d_misc.h, 16)
            d_misc.n += 16

        @blk.gpsimd
        def _(g):
            g.wait_ge(d_misc.h, d_misc.n)
            g.collective_compute("AllReduce", AL.add, replica_groups=[list(range(8))],
                                 ins=[cnt_in.ap().opt()], outs=[cnt_out.ap().opt()]).then_inc(c_ar.h)
            c_ar.n += 1

        cent_ctx.close()
        s_ev = [pa_ctx.enter_context(nc.sbuf_tensor(f"s_ev{i}", [128, 512], BF16)) for i in range(4)]
        s_gst = [pa_ctx.enter_context(nc.sbuf_tensor(f"s_gst{i}", [128, 512], F32)) for i in range(4)]

        pp_free = [0] * 4



        wslot_rel = [0, 0, 0]
        wseq = [0]

        def wload(src, fb):
            slot = wseq[0] % 3
            wseq[0] += 1

            @blk.gpsimd
            def _(g, src=src, fb=fb, slot=slot):
                if wslot_rel[slot] > 0:
                    g.wait_ge(m_pe.h, wslot_rel[slot])
                src_ap = src.rearrange("(c p) f -> p c f", p=128)[:, :, 512 * fb:512 * (fb + 1)]
                g.dma_start(out=s_w[slot][:, :, :], in_=src_ap).then_inc(d_w[slot].h, 16)
                d_w[slot].n += 16
            return (slot, d_w[slot].n)

        ev_gen = [0] * 4
        ev_rel = [0] * 4

        def evac_barrier():
            return [(d_ev[i], d_ev[i].n) for i in range(4)]

        def proj_fm(src_w, dst_dram, bias_pc, first=False, pre=None):
            cur_blk = pre if pre is not None else wload(src_w, 0)
            for fb in range(NFB):
                cur, need = cur_blk
                if fb + 1 < NFB:
                    nxt_blk = wload(src_w, fb + 1)
                for sub in range(4):
                    fc = fb * 4 + sub
                    pa_i, pb_i = (2 * sub) % 4, (2 * sub + 1) % 4
                    pA, pB = p_proj[pa_i], p_proj[pb_i]
                    ipair = []

                    @blk.tensor
                    def _(t, fb=fb, sub=sub, cur=cur, need=need, pA=pA, pB=pB,
                          pa_i=pa_i, pb_i=pb_i, first=first):
                        if sub == 0:
                            t.wait_ge(d_w[cur].h, need)
                            if first and fb == 0:
                                t.wait_ge(d_tp.h, 16 * NCH)
                        for ii in (pa_i, pb_i):
                            if pp_free[ii]:
                                t.wait_ge(m_dve.h, pp_free[ii])
                        for c in range(NCH):
                            lhs = s_w[cur][:, c, 128 * sub:128 * (sub + 1)]
                            t.matmul(pA[:, :], lhs, xn_fm[:, c, 0:512],
                                     start=(c == 0), stop=(c == NCH - 1))
                            i = t.matmul(pB[:, :], lhs, xn_fm[:, c, 512:1024],
                                         start=(c == 0), stop=(c == NCH - 1))
                        ipair.append(m_pe.inc(i))

                    for th, pp, ii in ((0, pA, pa_i), (1, pB, pb_i)):
                        es = (2 * fc + th) % 4

                        @blk.vector
                        def _(v, th=th, pp=pp, ii=ii, es=es, fc=fc, ipair=ipair,
                              bias_pc=bias_pc, first=first):
                            if first and fc == 0 and th == 0:
                                v.wait_ge(d_const.h, d_const.n)
                            v.wait_ge(m_pe.h, ipair[0])
                            if ev_gen[es] > 0:
                                v.wait_ge(d_ev[es].h, ev_rel[es])
                            i = v.tensor_scalar_add(s_ev[es][:, :], pp[:, :],
                                                    bias_pc[:, fc:fc + 1])
                            pp_free[ii] = m_dve.inc(i)

                        @blk.sync
                        def _(sy, th=th, es=es, fc=fc, dst_dram=dst_dram):
                            sy.wait_ge(m_dve.h, m_dve.n)
                            sy.dma_start(out=dst_dram[fc, :, 512 * th:512 * (th + 1)],
                                         in_=s_ev[es][:, :]).then_inc(d_ev[es].h, 16)
                            d_ev[es].n += 16
                            ev_rel[es] = d_ev[es].n
                            ev_gen[es] += 1
                wslot_rel[cur] = m_pe.n
                if fb + 1 < NFB:
                    cur_blk = nxt_blk

        k_ap = k_own.ap()
        proj_fm(wkT, k_ap, s_bkpc, first=True)
        kst_bar = evac_barrier()
        q_pre = wload(wqT, 0)

        @blk.gpsimd
        def _(g):
            for s_, v_ in kst_bar:
                g.wait_ge(s_.h, v_)
            g.collective_compute("AllGather", AL.bypass,
                                 replica_groups=[[0, 1], [2, 3], [4, 5], [6, 7]],
                                 ins=[k_own.ap().opt()], outs=[k_pair.ap().opt()]).then_inc(c_k.h)
            c_k.n += 1

        proj_fm(wqT, q_d, s_bqpc, pre=q_pre)
        qst_bar = evac_barrier()

        @blk.sync
        def _(sy):
            sy.wait_ge(c_ar.h, 1)
            sy.dma_start(out=s_cselc[:, :], in_=cnt_out.ap()).then_inc(d_misc.h, 16)
            d_misc.n += 16

        @blk.vector
        def _(v):
            v.wait_ge(d_misc.h, d_misc.n)
            i = v.tensor_scalar(s_csel[:, :], s_cselc[:, :], 0.5, None, op0=AL.is_gt)
            m_dve.inc(i)
        csel_dve = m_dve.n

        oh_mm = [0, 0]
        rsl_dve = [0, 0]
        for fb in range(NFB):
            sl = fb % 2

            @blk.gpsimd
            def _(g, fb=fb, sl=sl, all_misc=d_misc.n):
                if oh_mm[sl]:
                    g.wait_ge(m_pe.h, oh_mm[sl])
                else:
                    g.wait_ge(d_misc.h, all_misc)
                g.dma_start(out=s_oht[sl][:, :],
                            in_=asnrow_d[:, 512 * fb:512 * (fb + 1)].broadcast_to([64, 512])
                            ).then_inc(d_oh[sl].h, 16)
                d_oh[sl].n += 16

            @blk.vector
            def _(v, fb=fb, sl=sl):
                v.wait_ge(d_oh[sl].h, d_oh[sl].n)
                if fb == 0:
                    v.wait_ge(m_pool.h, 1)
                i = v.tensor_scalar(s_oht[sl][:, :], s_oht[sl][:, :], c_iota[:, :],
                                    None, op0=AL.is_equal)
                m_dve.inc(i)
            oh_dve = m_dve.n

            @blk.tensor
            def _(t, fb=fb, sl=sl, oh_dve=oh_dve, prev_copy=m_act.n):
                t.wait_ge(m_dve.h, oh_dve)
                if fb == 0:
                    t.wait_ge(m_dve.h, csel_dve)
                else:
                    t.wait_ge(m_act.h, prev_copy)
                i = t.matmul(p_rs[:, :], s_csel[:, :], s_oht[sl][:, :], start=True, stop=True)
                m_pe.inc(i)
            oh_mm[sl] = m_pe.n

            @blk.scalar
            def _(sc, fb=fb, sl=sl):
                sc.wait_ge(m_pe.h, oh_mm[sl])
                if rsl_dve[sl]:
                    sc.wait_ge(d_rsl[sl].h, rsl_dve[sl])
                i = sc.copy(s_rsl[sl][:, :], p_rs[:, :])
                m_act.inc(i)
            rsl_act = m_act.n

            @blk.sync
            def _(sy, fb=fb, sl=sl, rsl_act=rsl_act):
                sy.wait_ge(m_act.h, rsl_act)
                sy.dma_start(out=rsel_d[:, 512 * fb:512 * (fb + 1)],
                             in_=s_rsl[sl][:, :]).then_inc(d_rsl[sl].h, 16)
                d_rsl[sl].n += 16
                rsl_dve[sl] = d_rsl[sl].n
        rsl_bar = [(d_rsl[0], d_rsl[0].n), (d_rsl[1], d_rsl[1].n)]

        def proj_tm(src_w, dst_dram, out_f32, pre=None, brep=None, brep_need=0):
            if pre is not None:
                blocks = {0: pre[0], 1: pre[1]}
            else:
                blocks = {0: wload(src_w, 0), 1: wload(src_w, 1)}
            for wv in range(4):
                if 2 * wv + 2 < NFB:
                    blocks[2 * wv + 2] = wload(src_w, 2 * wv + 2)
                slA, nA = blocks[2 * wv]
                slB, nB = blocks[2 * wv + 1]
                for j in range(NT):
                    ui = wv * NT + j
                    idx = [(2 * ui) % 4, (2 * ui + 1) % 4]
                    ppair = [p_proj[idx[0]], p_proj[idx[1]]]
                    last = []

                    if j == 0:
                        @blk.tensor
                        def _(t, ppair=ppair, idx=idx, slA=slA, slB=slB,
                              nA=nA, nB=nB):
                            t.wait_ge(d_w[slA].h, nA)
                            for ii in idx:
                                if pp_free[ii]:
                                    t.wait_ge(m_dve.h, pp_free[ii])
                            for c in range(NCH):
                                t.matmul(ppair[0][:, :], xn_fm[:, c, 0:128], s_w[slA][:, c, :],
                                         start=(c == 0), stop=(c == NCH - 1))
                            t.wait_ge(d_w[slB].h, nB)
                            for c in range(NCH):
                                i = t.matmul(ppair[1][:, :], xn_fm[:, c, 0:128], s_w[slB][:, c, :],
                                             start=(c == 0), stop=(c == NCH - 1))
                            last.append(m_pe.inc(i))
                    else:
                        @blk.tensor
                        def _(t, j=j, ppair=ppair, idx=idx, slA=slA, slB=slB):
                            for ii in idx:
                                if pp_free[ii]:
                                    t.wait_ge(m_dve.h, pp_free[ii])
                            for c in range(NCH):
                                lhs = xn_fm[:, c, 128 * j:128 * (j + 1)]
                                t.matmul(ppair[0][:, :], lhs, s_w[slA][:, c, :],
                                         start=(c == 0), stop=(c == NCH - 1))
                                i = t.matmul(ppair[1][:, :], lhs, s_w[slB][:, c, :],
                                             start=(c == 0), stop=(c == NCH - 1))
                            last.append(m_pe.inc(i))

                    for k in range(2):
                        fb = wv * 2 + k
                        es = (2 * ui + k) % 4

                        @blk.vector
                        def _(v, k=k, fb=fb, es=es, ppair=ppair, idx=idx, last=last,
                              out_f32=out_f32, wv=wv, j=j):
                            v.wait_ge(m_pe.h, last[0])
                            if ev_gen[es] > 0:
                                v.wait_ge(d_ev[es].h, ev_rel[es])
                            if wv == 0 and j == 0 and k == 0:
                                v.wait_ge(d_brow.h, brep_need)
                            dst = s_gst[es] if out_f32 else s_ev[es]
                            i = v.tensor_add(dst[:, :], ppair[k][:, :],
                                             s_brep[:, 512 * fb:512 * (fb + 1)])
                            pp_free[idx[k]] = m_dve.inc(i)

                        if out_f32:
                            @blk.scalar
                            def _(sc, es=es):
                                sc.wait_ge(m_dve.h, m_dve.n)
                                i = sc.activation(s_gst[es][:, :], s_gst[es][:, :], AF.Silu)
                                m_act.inc(i)

                        @blk.sync
                        def _(sy, k=k, fb=fb, es=es, j=j, dst_dram=dst_dram, out_f32=out_f32):
                            if out_f32:
                                sy.wait_ge(m_act.h, m_act.n)
                            else:
                                sy.wait_ge(m_dve.h, m_dve.n)
                            src = s_gst[es] if out_f32 else s_ev[es]
                            sy.dma_start(out=dst_dram[j, :, 512 * fb:512 * (fb + 1)],
                                         in_=src[:, :]).then_inc(d_ev[es].h, 16)
                            d_ev[es].n += 16
                            ev_rel[es] = d_ev[es].n
                            ev_gen[es] += 1
                wslot_rel[slA] = m_pe.n
                wslot_rel[slB] = m_pe.n
                if 2 * wv + 3 < NFB:
                    blocks[2 * wv + 3] = wload(src_w, 2 * wv + 3)

        proj_tm(wvT, v_own.ap(), out_f32=False, brep_need=brep_need)
        vst_bar = evac_barrier()
        f_pre = (wload(wfT, 0), wload(wfT, 1))

        @blk.gpsimd
        def _(g):
            for s_, v_ in vst_bar:
                g.wait_ge(s_.h, v_)
            g.collective_compute("AllGather", AL.bypass,
                                 replica_groups=[[0, 1], [2, 3], [4, 5], [6, 7]],
                                 ins=[v_own.ap().opt()], outs=[v_pair.ap().opt()]).then_inc(c_v.h)
            c_v.n += 1

        proj_tm(wfT, g_d, out_f32=True, pre=f_pre, brep_need=brep_need)
        gst_bar = evac_barrier() + rsl_bar
        ffn_end_pe = m_pe.n
        pa_evac_dve = m_dve.n

        pa_ctx.close()
        xn_ctx.close()
        sw_ctx.close()

        @blk.gpsimd
        def _(g, pa_dve=0):
            g.wait_ge(m_dve.h, pa_evac_dve)
            g.dma_start(out=s_brep[:, :], in_=bo_r.broadcast_to([128, D])).then_inc(d_brow.h, 16)
            d_brow.n += 16
        brep_o_need = d_brow.n

        wo_early_ctx = ExitStack()
        s_wo01 = [wo_early_ctx.enter_context(nc.sbuf_tensor(f"s_wo{i}", [128, NCH, 512], BF16))
                  for i in range(2)]
        s_mrep = wo_early_ctx.enter_context(nc.sbuf_tensor("s_mrep", [128, D], BF16))

        @blk.gpsimd
        def _(g):
            for s_, v_ in rsl_bar:
                g.wait_ge(s_.h, v_)
            g.dma_start(out=s_mrep[:, :], in_=rsel_d.broadcast_to([128, D])).then_inc(d_mrep.h, 16)
            d_mrep.n += 16
        mrep_need = d_mrep.n

        @blk.gpsimd
        def _(g):
            g.wait_ge(m_pe.h, ffn_end_pe)
            for s_, v_ in gst_bar:
                g.wait_ge(s_.h, v_)
            for i_ in range(2):
                src_ap = woT.rearrange("(c p) f -> p c f", p=128)[:, :, 512 * i_:512 * (i_ + 1)]
                g.dma_start(out=s_wo01[i_][:, :, :], in_=src_ap).then_inc(d_wo4[i_].h, 16)
                d_wo4[i_].n += 16
        wo01_need = [d_wo4[0].n, d_wo4[1].n]

        at_ctx = ExitStack()
        AS = lambda name, shape, dt: at_ctx.enter_context(nc.sbuf_tensor(name, shape, dt))
        AP_ = lambda name, shape: at_ctx.enter_context(nc.psum_tensor(name, shape, F32))
        s_masks = AS("s_masks", [128, 8, 512], BF16)
        s_q = [AS("s_q0", [128, TOK], BF16), AS("s_q1", [128, TOK], BF16)]

        @blk.sync
        def _(sy):
            for s_, v_ in gst_bar:
                sy.wait_ge(s_.h, v_)
            sy.dma_start(out=s_masks[:, :, :], in_=amask).then_inc(d_misc.h, 16)
            d_misc.n += 16
        masks_need = d_misc.n
        s_k = [AS("s_k0", [128, 2048], BF16), AS("s_k1", [128, 2048], BF16)]
        s_v = [AS("s_v0", [128, 2048], BF16), AS("s_v1", [128, 2048], BF16)]
        s_exp = [AS(f"s_exp{i}", [128, 1024], BF16) for i in range(4)]
        s_ao = [AS("s_ao0", [128, TOK], BF16), AS("s_ao1", [128, TOK], BF16)]
        s_r = [AS("s_r0", [1, 512], F32), AS("s_r1", [1, 512], F32)]
        s_rr = [AS("s_rr0", [128, 512], F32), AS("s_rr1", [128, 512], F32)]
        p_s = [AP_("p_s0", [128, 1024]), AP_("p_s1", [128, 1024])]
        p_ao = [AP_("p_ao0", [128, 512]), AP_("p_ao1", [128, 512])]
        p_d = AP_("p_d", [1, 512])

        head_rel = [0, 0]
        PF_N = 5

        def prefetch_head(h):
            slot = h % 2

            @blk.sync
            def _(sy, h=h, slot=slot):
                if h <= 1:
                    sy.wait_ge(c_k.h, 1)
                    sy.wait_ge(c_v.h, 1)
                    for s_, v_ in qst_bar:
                        sy.wait_ge(s_.h, v_)
                    for s_, v_ in gst_bar:
                        sy.wait_ge(s_.h, v_)
                else:
                    sy.wait_ge(m_pe.h, head_rel[slot])
                sy.dma_start(out=s_q[slot][:, :], in_=q_d[h]).then_inc(d_pf[slot].h, 16)
                for g_ in range(2):
                    sy.dma_start(out=s_k[slot][:, 1024 * g_:1024 * (g_ + 1)],
                                 in_=k_pair.ap()[g_, h]).then_inc(d_pf[slot].h, 16)
                    vsrc = v_pair.ap()[g_, :, :, 128 * h:128 * (h + 1)].rearrange("t p f -> p t f")
                    vdst = s_v[slot][:, 1024 * g_:1024 * (g_ + 1)].rearrange("p (t f) -> p t f", f=128)
                    sy.dma_start(out=vdst, in_=vsrc).then_inc(d_pf[slot].h, 16)
                d_pf[slot].n += 16 * PF_N

        prefetch_head(0)
        prefetch_head(1)

        ps_rel_tok = [None, None]
        exp_slot_rel = [0, 0, 0, 0]
        pao_rel = [0, 0]
        ao_slot_rel = [0, 0]
        uid = 0
        LAG = 3
        pend = []
        deferred = []
        prev_r_dve = [0]
        r_slot_rel = [0, 0]
        rr_slot_rel = [0, 0]

        def finish_block(rec):
            (my, sb, qoff, exp_ready, kt0, kt1, u, aops, slot, npair, ao_i, bq, b_id, hh) = rec
            rb = b_id % 2
            pe_block_done = m_pe.n

            @blk.vector
            def _(v, rb=rb, pe_block_done=pe_block_done):
                v.wait_ge(m_pe.h, pe_block_done)
                if r_slot_rel[rb]:
                    v.wait_ge(d_rr[rb].h, r_slot_rel[rb])
                i = v.reciprocal(s_r[rb][:, :], p_d[:, :])
                m_dve.inc(i)
            r_dve = m_dve.n
            prev_r_dve[0] = r_dve

            @blk.sync
            def _(sy, rb=rb, r_dve=r_dve, rr_rel=rr_slot_rel[rb]):
                sy.wait_ge(m_dve.h, r_dve)
                sy.dma_start(out=r_d[rb], in_=s_r[rb][:, :]).then_inc(d_rr[rb].h, 16)
                d_rr[rb].n += 16
                sy.wait_ge(d_rr[rb].h, d_rr[rb].n)
                if rr_rel:
                    sy.wait_ge(m_dve.h, rr_rel)
                sy.dma_start(out=s_rr[rb][:, :],
                             in_=r_d[rb].broadcast_to([128, 512])).then_inc(d_rr[rb].h, 16)
                d_rr[rb].n += 16
            rr_need = d_rr[rb].n
            r_slot_rel[rb] = rr_need

            def emit_evac(rb=rb, rr_need=rr_need, aops=aops, slot=slot, bq=bq,
                          ao_i=ao_i, hh=hh):
                @blk.vector
                def _(v):
                    v.wait_ge(d_rr[rb].h, rr_need)
                    if ao_slot_rel[slot]:
                        v.wait_ge(d_ao[slot].h, ao_slot_rel[slot])
                    i = v.tensor_mul(s_ao[slot][:, 512 * bq:512 * (bq + 1)],
                                     aops[:, :], s_rr[rb][:, :])
                    m_dve.inc(i)
                pao_rel[ao_i] = m_dve.n
                rr_slot_rel[rb] = m_dve.n

                if bq == 1:
                    ao_dve = m_dve.n

                    @blk.sync
                    def _(sy):
                        sy.wait_ge(m_dve.h, ao_dve)
                        sy.dma_start(out=ao_d[:, :, hh, :].rearrange("j p f -> p j f"),
                                     in_=s_ao[slot][:, :].rearrange(
                                         "p (j f) -> p j f", f=128)).then_inc(d_ao[slot].h, 16)
                        d_ao[slot].n += 16
                        ao_slot_rel[slot] = d_ao[slot].n
            deferred.append(emit_evac)

        def emit_dav(rec):
            (my, sb, qoff, exp_ready, kt0, kt1, u, aops, slot, npair, ao_i, bq, b_id, hh) = rec

            @blk.tensor
            def _(t):
                t.wait_ge(exp_ready[0].h, exp_ready[1])
                if u == 0:
                    if pao_rel[ao_i]:
                        t.wait_ge(m_dve.h, pao_rel[ao_i])
                    if prev_r_dve[0]:
                        t.wait_ge(m_dve.h, prev_r_dve[0])
                for half in (0, 1):
                    kk = 2 * u + half
                    t.matmul(p_d[:, :], ones_col[:, :],
                             s_exp[sb][:, qoff + 512 * half:qoff + 512 * (half + 1)],
                             start=(kk == 0), stop=(kk == 2 * npair - 1))
                for half, (g_, tt_) in ((0, kt0), (1, kt1)):
                    kk = 2 * u + half
                    col = 1024 * g_ + 128 * tt_
                    i = t.matmul(aops[:, :], s_v[slot][:, col:col + 128],
                                 s_exp[sb][:, qoff + 512 * half:qoff + 512 * (half + 1)],
                                 start=(kk == 0), stop=(kk == 2 * npair - 1))
                m_pe.inc(i)
            exp_slot_rel[sb] = m_pe.n

            if u == npair - 1:
                finish_block(rec)
                if bq == 1:
                    head_rel[hh % 2] = m_pe.n
                    if hh + 2 < H:
                        prefetch_head(hh + 2)

        b_id = 0
        batch = 0
        for h in range(H):
            slot = h % 2
            for Bq in range(2):
                kts = [(g_, t) for t in range(4 * (Bq + 1)) for g_ in (0, 1)]
                npair = len(kts) // 2
                ao_i = (2 * h + Bq) % 2
                aops = p_ao[ao_i]

                for u in range(npair):
                    kt0, kt1 = kts[2 * u], kts[2 * u + 1]
                    su = uid % 2
                    su4 = uid % 4
                    my = uid
                    uid += 1

                    @blk.tensor
                    def _(t, h=h, Bq=Bq, u=u, su=su, slot=slot, kt0=kt0, kt1=kt1):
                        if u == 0 and Bq == 0:
                            t.wait_ge(d_pf[slot].h, 16 * PF_N * (h // 2 + 1))
                        if ps_rel_tok[su] is not None:
                            ctr, val = ps_rel_tok[su]
                            t.wait_ge(ctr.h, val)
                        else:
                            t.wait_ge(m_dve.h, pa_evac_dve)
                        for half, (g_, tt_) in ((0, kt0), (1, kt1)):
                            col = 1024 * g_ + 128 * tt_
                            i = t.matmul(p_s[su][:, 512 * half:512 * (half + 1)],
                                         s_k[slot][:, col:col + 128],
                                         s_q[slot][:, 512 * Bq:512 * (Bq + 1)],
                                         start=True, stop=True)
                        m_pe.inc(i)

                    pe_scores = m_pe.n

                    while deferred:
                        deferred.pop(0)()
                    if len(pend) >= LAG:
                        emit_dav(pend.pop(0))

                    @blk.scalar
                    def _(s, su=su, su4=su4, my=my, pe_scores=pe_scores):
                        s.wait_ge(m_pe.h, pe_scores)
                        if my >= 4:
                            s.wait_ge(m_pe.h, exp_slot_rel[su4])
                        i = s.activation(s_exp[su4][:, :], p_s[su][:, :], AF.Exp, scale=INVSQ)
                        m_act.inc(i)
                    ps_rel_tok[su] = (m_act, m_act.n)
                    exp_ready = (m_act, m_act.n)

                    masked = []
                    for half, (g_, tt_) in ((0, kt0), (1, kt1)):
                        if tt_ >= 4 * Bq:
                            masked.append((half, g_ * 4 + (tt_ - 4 * Bq)))
                    if masked:
                        @blk.vector
                        def _(v, su4=su4, masked=masked, exp_ready=exp_ready,
                              first_mask=(uid == 1)):
                            if first_mask:
                                v.wait_ge(d_misc.h, masks_need)
                            v.wait_ge(exp_ready[0].h, exp_ready[1])
                            for half, mi in masked:
                                i = v.tensor_mul(s_exp[su4][:, 512 * half:512 * (half + 1)],
                                                 s_exp[su4][:, 512 * half:512 * (half + 1)],
                                                 s_masks[:, mi, :])
                            m_dve.inc(i)
                        exp_ready = (m_dve, m_dve.n)

                    pend.append((my, su4, 0, exp_ready, kt0, kt1, u, aops, slot,
                                 npair, ao_i, Bq, b_id, h))
                b_id += 1

        while pend:
            emit_dav(pend.pop(0))
            while deferred:
                deferred.pop(0)()
        while deferred:
            deferred.pop(0)()

        aot_all_bar = [(d_ao[i], d_ao[i].n) for i in range(2)]
        at_ctx.close()

        attn_end_dve = m_dve.n

        wo_ctx = ExitStack()
        WS = lambda name, shape, dt: wo_ctx.enter_context(nc.sbuf_tensor(name, shape, dt))
        s_wo = s_wo01 + [WS(f"s_wo{i}", [128, NCH, 512], BF16) for i in range(2, 4)]
        s_aot = [WS("s_aot0", [128, H, 128], BF16), WS("s_aot1", [128, H, 128], BF16)]
        NCR = 3
        s_g2 = [WS(f"s_g2{i}", [128, 512], F32) for i in range(NCR)]
        s_x2 = [WS(f"s_x2{i}", [128, 512], F32) for i in range(NCR)]
        s_sil = [WS(f"s_sil{i}", [128, 512], F32) for i in range(NCR)]
        s_sv = [WS(f"s_sv{i}", [128, 512], F32) for i in range(NCR)]
        p_wo = [wo_ctx.enter_context(nc.psum_tensor(f"p_wo{i}", [128, 512], F32))
                for i in range(4)]

        wo_rel = [0, 0, 0, 0]

        def load_woblock(fb, slot):
            @blk.gpsimd
            def _(g, fb=fb, slot=slot):
                if wo_rel[slot] == 0:
                    g.wait_ge(m_dve.h, attn_end_dve)
                    for s_, v_ in aot_all_bar:
                        g.wait_ge(s_.h, v_)
                else:
                    g.wait_ge(m_pe.h, wo_rel[slot])
                src_ap = woT.rearrange("(c p) f -> p c f", p=128)[:, :, 512 * fb:512 * (fb + 1)]
                g.dma_start(out=s_wo[slot][:, :, :], in_=src_ap).then_inc(d_wo4[slot].h, 16)
                d_wo4[slot].n += 16
            return d_wo4[slot].n

        aot_gen = [0, 0]
        aot_rel = [0, 0]
        cg_gen = [0] * NCR
        cg_rel = [0] * NCR
        out_slot_rel = [0, 0, 0]
        sil_rel = [0] * NCR
        sv_rel = [0] * NCR
        pwo_free = [0] * 4

        wo_need = {0: wo01_need[0], 1: wo01_need[1]}
        cri = 0

        for wv in range(4):
            sl0, sl1 = (2 * wv) % 4, (2 * wv + 1) % 4
            if 2 * wv + 2 < NFB:
                wo_need[(2 * wv + 2) % 4] = load_woblock(2 * wv + 2, (2 * wv + 2) % 4)
            if 2 * wv + 3 < NFB:
                wo_need[(2 * wv + 3) % 4] = load_woblock(2 * wv + 3, (2 * wv + 3) % 4)
            for j in range(NT):
                aslot = j % 2

                @blk.scalar
                def _(sy, wv=wv, j=j, aslot=aslot):
                    if aot_gen[aslot] > 0:
                        sy.wait_ge(m_pe.h, aot_rel[aslot])
                    else:
                        sy.wait_ge(m_dve.h, attn_end_dve)
                        for s_, v_ in aot_all_bar:
                            sy.wait_ge(s_.h, v_)
                    sy.dma_start(out=s_aot[aslot][:, :, :],
                                 in_=ao_d[j]).then_inc(d_aot[aslot].h, 16)
                    d_aot[aslot].n += 16

                aot_gen[aslot] += 1
                aot_need = d_aot[aslot].n

                crs = [cri % NCR, (cri + 1) % NCR]
                cri += 2
                cg_need = {}
                for k in range(2):
                    fb = 2 * wv + k
                    cs = crs[k]

                    @blk.scalar
                    def _(sy, j=j, fb=fb, cs=cs, wv=wv):
                        if cg_gen[cs] > 0:
                            sy.wait_ge(m_pool.h, cg_rel[cs])
                        else:
                            sy.wait_ge(m_dve.h, attn_end_dve)
                            for s_, v_ in aot_all_bar:
                                sy.wait_ge(s_.h, v_)
                            for s_, v_ in gst_bar:
                                sy.wait_ge(s_.h, v_)
                        sy.dma_start(out=s_g2[cs][:, :],
                                     in_=g_d[j, :, 512 * fb:512 * (fb + 1)]).then_inc(d_cg[cs].h, 16)
                        sy.dma_start(out=s_x2[cs][:, :],
                                     in_=x_in[j, :, 512 * fb:512 * (fb + 1)]).then_inc(d_cg[cs].h, 16)
                        d_cg[cs].n += 32
                    cg_gen[cs] += 1
                    cg_need[cs] = d_cg[cs].n

                pA, pB = p_wo[(2 * j) % 4], p_wo[(2 * j + 1) % 4]
                iA, iB = (2 * j) % 4, (2 * j + 1) % 4

                @blk.tensor
                def _(t, wv=wv, j=j, aslot=aslot, sl0=sl0, sl1=sl1, pA=pA, pB=pB,
                      iA=iA, iB=iB, aot_need=aot_need,
                      n0=wo_need[sl0], n1=wo_need[sl1]):
                    if j == 0:
                        t.wait_ge(d_wo4[sl0].h, n0)
                        t.wait_ge(d_wo4[sl1].h, n1)
                        if wv == 0:
                            t.wait_ge(m_dve.h, attn_end_dve)
                    t.wait_ge(d_aot[aslot].h, aot_need)
                    for ii in (iA, iB):
                        if pwo_free[ii]:
                            t.wait_ge(m_dve.h, pwo_free[ii])
                    for c in range(NCH):
                        lhs = s_aot[aslot][:, c, :]
                        t.matmul(pA[:, :], lhs, s_wo[sl0][:, c, :],
                                 start=(c == 0), stop=(c == NCH - 1))
                        i = t.matmul(pB[:, :], lhs, s_wo[sl1][:, c, :],
                                     start=(c == 0), stop=(c == NCH - 1))
                    m_pe.inc(i)
                aot_rel[aslot] = m_pe.n
                wo_pe = m_pe.n

                sv_ready = {}
                for k, (pp, ip) in enumerate(((pA, iA), (pB, iB))):
                    fb = 2 * wv + k
                    os_ = crs[k]

                    @blk.vector
                    def _(v, pp=pp, fb=fb, os_=os_, wo_pe=wo_pe):
                        v.wait_ge(m_pe.h, wo_pe)
                        if sv_rel[os_]:
                            v.wait_ge(m_pool.h, sv_rel[os_])
                        if out_slot_rel[os_]:
                            v.wait_ge(d_out[os_].h, out_slot_rel[os_])
                        if wv == 0 and j == 0:
                            v.wait_ge(d_brow.h, brep_o_need)
                        i = v.tensor_add(s_sv[os_][:, :], pp[:, :],
                                         s_brep[:, 512 * fb:512 * (fb + 1)])
                        m_dve.inc(i)
                    pwo_free[ip] = m_dve.n
                    sv_ready[k] = m_dve.n

                for k in range(2):
                    fb = 2 * wv + k
                    cs = crs[k]
                    os_ = cs

                    @blk.vector
                    def _(v, j=j, fb=fb, os_=os_, cs=cs, cg_need=cg_need[cs],
                          first=(wv == 0 and j == 0 and k == 0)):
                        v.wait_ge(d_cg[cs].h, cg_need)
                        if sil_rel[os_]:
                            v.wait_ge(m_pool.h, sil_rel[os_])
                        if first:
                            v.wait_ge(d_mrep.h, mrep_need)
                        i = v.scalar_tensor_tensor(s_sil[os_][:, :], s_g2[cs][:, :],
                                                   s_qsel[:, j:j + 1],
                                                   s_mrep[:, 512 * fb:512 * (fb + 1)],
                                                   op0=AL.mult, op1=AL.mult)
                        m_dve.inc(i)
                    dve_cmb = m_dve.n

                    @blk.gpsimd
                    def _(g, os_=os_, cs=cs, dve_cmb=dve_cmb, sv_ready=sv_ready[k]):
                        g.wait_ge(m_dve.h, dve_cmb)
                        g.wait_ge(m_dve.h, sv_ready)
                        i = g.tensor_mul(s_sv[os_][:, :], s_sv[os_][:, :], s_sil[os_][:, :])
                        m_pool.inc(i)
                        sil_rel[os_] = m_pool.n
                        g.wait_ge(m_pool.h, m_pool.n)
                        i = g.tensor_add(s_sv[os_][:, :], s_sv[os_][:, :], s_x2[cs][:, :])
                        m_pool.inc(i)
                    sv_rel[os_] = m_pool.n
                    cg_rel[cs] = m_pool.n
                    pool_cmb = m_pool.n

                    @blk.sync
                    def _(sy, j=j, fb=fb, os_=os_, pool_cmb=pool_cmb):
                        sy.wait_ge(m_pool.h, pool_cmb)
                        sy.dma_start(out=y_out[j, :, 512 * fb:512 * (fb + 1)],
                                     in_=s_sv[os_][:, :]).then_inc(d_out[os_].h, 16)
                        d_out[os_].n += 16
                        out_slot_rel[os_] = d_out[os_].n

            wo_rel[sl0] = m_pe.n
            wo_rel[sl1] = m_pe.n

        @blk.sync
        def _(sy):
            for i_ in range(3):
                sy.wait_ge(d_out[i_].h, d_out[i_].n)

        wo_ctx.close()
        wo_early_ctx.close()

    return nc


_NC_CACHE = None


def _get_nc():
    global _NC_CACHE
    if _NC_CACHE is None:
        _NC_CACHE = build_nc()
    return _NC_CACHE


def _build_masks(r):
    kk = np.arange(128)[:, None]
    qq = np.arange(512)[None, :]
    out = np.zeros((8, 128, 512), np.float32)
    for g in range(2):
        for m in range(4):
            delta = 1 if (g == 1 and r == 0) else 0
            out[g * 4 + m] = (kk <= qq - 128 * m - delta).astype(np.float32)
    return np.ascontiguousarray(out.transpose(1, 0, 2)).astype(bf16)


def kernel(L0_attn_x, L0_attn_Wq, L0_attn_bq, L0_attn_Wk, L0_attn_bk,
           L0_attn_Wv, L0_attn_bv, L0_attn_Wo, L0_attn_bo,
           L0_ffn_weight, L0_ffn_bias, L0_ffn_centroids, L0_ffn_assignments):
    x = np.ascontiguousarray(np.asarray(L0_attn_x, dtype=np.float32))
    wqT = np.ascontiguousarray(np.asarray(L0_attn_Wq, np.float32).T)
    wkT = np.ascontiguousarray(np.asarray(L0_attn_Wk, np.float32).T)
    wvT = np.ascontiguousarray(np.asarray(L0_attn_Wv, np.float32).T)
    woT = np.ascontiguousarray(np.asarray(L0_attn_Wo, np.float32).T)
    wfT = np.ascontiguousarray(np.asarray(L0_ffn_weight, np.float32).T)
    centT = np.ascontiguousarray(np.asarray(L0_ffn_centroids, np.float32).T)
    bq = np.asarray(L0_attn_bq, np.float32)
    bk = np.asarray(L0_attn_bk, np.float32)
    bv = np.ascontiguousarray(np.asarray(L0_attn_bv, np.float32).reshape(1, D))
    bo = np.ascontiguousarray(np.asarray(L0_attn_bo, np.float32).reshape(1, D))
    bf_ = np.ascontiguousarray(np.asarray(L0_ffn_bias, np.float32).reshape(1, D))
    assign = np.ascontiguousarray(np.asarray(L0_ffn_assignments, np.int32).reshape(1, D))
    bq_pc = np.ascontiguousarray(bq.reshape(NCH, 128).T)
    bk_pc = np.ascontiguousarray(bk.reshape(NCH, 128).T)

    nc = _get_nc()
    in_maps = []
    for c in range(8):
        b, r = c // 2, c % 2
        x_own = np.ascontiguousarray(x[b, r::2, :]).reshape(NT, 128, D)
        in_maps.append({
            "x_own": x_own, "wqT": wqT, "wkT": wkT, "wvT": wvT, "woT": woT,
            "wfT": wfT, "centT": centT, "bq_pc": bq_pc, "bk_pc": bk_pc,
            "bv_r": bv, "bo_r": bo, "bf_r": bf_, "assign": assign,
            "amask": _build_masks(r),
        })
    res = run_bass_kernel_spmd(nc, in_maps, core_ids=list(range(8)))
    out = np.empty((B, L, D), np.float32)
    for c in range(8):
        b, r = c // 2, c % 2
        out[b, r::2, :] = res.results[c]["y_own"].reshape(TOK, D)
    return out
